# version 02
# baseline (speedup 1.0000x reference)
import numpy as np
import concourse.bass as bass
import concourse.mybir as mybir
from contextlib import ExitStack

F32 = mybir.dt.float32
BF16 = mybir.dt.bfloat16
I32 = mybir.dt.int32
AF = mybir.ActivationFunctionType
ALU = mybir.AluOpType
AX = mybir.AxisListType

T = 4096
D = 1024
NE = 8
DF = 4096
EPS = 1e-5


class _StubInst:
    def then_inc(self, *a, **k):
        return self


class _StubEng:
    def __getattr__(self, name):
        def f(*a, **k):
            return _StubInst()
        return f


class Sem:
    def __init__(self, es, nc, name):
        self.h = es.enter_context(nc.semaphore(name))
        self.name = name
        self.v = 0


class Build:

    def __init__(self, nc, es):
        self.nc = nc
        self.es = es
        self.ck = {}
        self.emit = False

    def sem(self, name):
        return Sem(self.es, self.nc, name)

    def inc(self, sem, inst, key=None, n=1):
        if self.emit:
            inst.then_inc(sem.h, n)
            assert self.ck.get(key, (sem, sem.v + n))[1] == sem.v + n, key
        sem.v += n
        if key is not None:
            self.ck[key] = (sem, sem.v)
        return inst

    def wait(self, eng, key):
        if key not in self.ck:
            assert not self.emit, f"missing ck {key}"
            return
        sem, v = self.ck[key]
        if self.emit:
            eng.wait_ge(sem.h, v)

    def wait_val(self, eng, sem, v):
        if self.emit:
            eng.wait_ge(sem.h, v)

    def gsync(self, eng, sem, inst):
        self.inc(sem, inst)
        if self.emit:
            eng.wait_ge(sem.h, sem.v)
        return inst

    def run(self, blk):
        self.emit = False
        for body in self.bodies():
            body(_StubEng())
        for s in self.__dict__.values():
            if isinstance(s, Sem):
                s.v = 0
        self.emit = True
        sections = {
            "sync": blk.sync, "gpsimd": blk.gpsimd, "vector": blk.vector,
            "scalar": blk.scalar, "tensor": blk.tensor,
        }
        for body in self.bodies():
            sections[body.engine_name](body)


def engine(name):
    def deco(f):
        f.engine_name = name
        return f
    return deco



class FFN0(Build):
    NT = T // 128
    NT8 = T // 256

    def __init__(self, nc, es, debug=False):
        super().__init__(nc, es)
        self.debug = debug
        E = es.enter_context
        self.x_d = nc.dram_tensor("x", [T, D], F32, kind="ExternalInput").ap()
        self.w1_d = nc.dram_tensor("w1", [NE, D, DF], F32, kind="ExternalInput").ap()
        self.w2_d = nc.dram_tensor("w2", [NE, DF, D], F32, kind="ExternalInput").ap()
        self.wr_d = nc.dram_tensor("wr_l", [128, 8, 8], F32, kind="ExternalInput").ap()
        self.b1_d = nc.dram_tensor("b1_l", [128, 8, 32], F32, kind="ExternalInput").ap()
        self.b2_d = nc.dram_tensor("b2_l", [128, 8, 8], F32, kind="ExternalInput").ap()
        self.esel_d = nc.dram_tensor("esel", [8, 1024], BF16, kind="ExternalInput").ap()
        self.idb_d = nc.dram_tensor("identb_d", [128, 128], BF16, kind="ExternalInput").ap()
        self.idf_d = nc.dram_tensor("identf_d", [128, 128], F32, kind="ExternalInput").ap()
        self.xo_d = nc.dram_tensor("xo", [T, D], F32, kind="ExternalOutput").ap()
        kind = "ExternalOutput" if debug else "Internal"
        self.zfm_d = nc.dram_tensor("zfm", [128, 8, T], BF16, kind=kind)
        self.acc_d = nc.dram_tensor("acc", [8, 128, T], BF16, kind=kind)
        if debug:
            self.self_d = nc.dram_tensor("selfm_o", [8, T], BF16,
                                         kind="ExternalOutput").ap()

        self.xt = E(nc.sbuf_tensor("xt", [128, 2, D], F32))
        self.scr = E(nc.sbuf_tensor("scr", [128, 256], F32))
        self.scr2 = E(nc.sbuf_tensor("scr2", [128, D], BF16))
        self.ztile = E(nc.sbuf_tensor("ztile", [128, D], BF16))
        self.zsb = E(nc.sbuf_tensor("zsb", [128, 8, 256], BF16))
        self.cols = E(nc.sbuf_tensor("cols", [128, 16], F32))
        self.epsc = E(nc.sbuf_tensor("epsc", [128, 1], F32))
        self.probs8 = E(nc.sbuf_tensor("probs8", [128, 8], F32))
        self.ohsel = E(nc.sbuf_tensor("ohsel", [128, 8], F32))
        self.selw = E(nc.sbuf_tensor("selw", [128, 8], F32))
        self.selfm = E(nc.sbuf_tensor("selfm", [8, T], BF16))
        self.eselsb = E(nc.sbuf_tensor("eselsb", [8, 1024], BF16))
        self.wrsb = E(nc.sbuf_tensor("wrsb", [128, 8, 8], BF16))
        self.b1sb = E(nc.sbuf_tensor("b1sb", [128, 8, 32], F32))
        self.b2sb = E(nc.sbuf_tensor("b2sb", [128, 8, 8], F32))
        self.identb = E(nc.sbuf_tensor("identb", [128, 128], BF16))
        self.identf = E(nc.sbuf_tensor("identf", [128, 128], F32))
        self.w1sb = E(nc.sbuf_tensor("w1sb", [128, 8, DF], BF16))
        self.w2sb = E(nc.sbuf_tensor("w2sb", [128, 32, D], BF16))
        self.zt = E(nc.sbuf_tensor("zt", [128, 2, 8, 256], BF16))
        self.hsb = E(nc.sbuf_tensor("hsb", [128, 32, 256], BF16))
        self.scbis = self.hsb[:].rearrange("p a b -> p (a b)")[:, 0:2048].bitcast(FP16)
        self.ysb = E(nc.sbuf_tensor("ysb", [128, 2, 8, 256], BF16))
        self.selb = E(nc.sbuf_tensor("selb", [128, 256], F32))
        self.accsb = E(nc.sbuf_tensor("accsb", [128, 1, 8, 128], BF16))

        self.pb0 = E(nc.psum_tensor("pb0", [128, 1024], BF16))
        self.pb1 = E(nc.psum_tensor("pb1", [128, 512], F32))
        self.pb2 = E(nc.psum_tensor("pb2", [128, 512], F32))
        self.pb3 = E(nc.psum_tensor("pb3", [128, 512], F32))
        self.pb4 = E(nc.psum_tensor("pb4", [128, 512], F32))
        self.pb5 = E(nc.psum_tensor("pb5", [128, 512], F32))
        self.pb6 = E(nc.psum_tensor("pb6", [128, 512], F32))

        self.dC = self.sem("dC")
        self.dX = self.sem("dX")
        self.dZw = self.sem("dZw")
        self.dZl = self.sem("dZl")
        self.dW = self.sem("dW")
        self.dA = self.sem("dA")
        self.dAl = self.sem("dAl")
        self.dXo = self.sem("dXo")
        self.sS = self.sem("sS")
        self.sV = self.sem("sV")
        self.sT = self.sem("sT")

    def bodies(self):
        return [self.b_sync, self.b_gpsimd, self.b_vector, self.b_scalar,
                self.b_tensor]

    @engine("sync")
    def b_sync(self, sync):
        s = self
        s.inc(s.dC, sync.dma_start(out=s.b1sb[:], in_=s.b1_d), None, 16)
        s.inc(s.dC, sync.dma_start(out=s.b2sb[:], in_=s.b2_d), None, 16)
        s.inc(s.dC, sync.dma_start(out=s.eselsb[:], in_=s.esel_d), None, 16)
        s.inc(s.dC, sync.dma_start(out=s.identb[:], in_=s.idb_d), None, 16)
        s.inc(s.dC, sync.dma_start(out=s.identf[:], in_=s.idf_d), ("consts",), 16)
        for t in range(s.NT):
            if t >= 2:
                s.wait(sync, ("zdone", t - 2))
                s.wait(sync, ("sq", t - 2))
            s.inc(s.dX, sync.dma_start(
                out=s.xt[:, t % 2, :], in_=s.x_d[t * 128:(t + 1) * 128, :]),
                ("xload", t), 16)
            if t % 2 == 1:
                s.wait(sync, ("zsb", t))
                s.inc(s.dZw, sync.dma_start(
                    out=s.zfm_d.ap()[:, :, (t - 1) * 128:(t + 1) * 128],
                    in_=s.zsb[:]), ("zw", t // 2), 16)
        for e in range(NE):
            for t8 in range(s.NT8):
                i = e * s.NT8 + t8
                if i >= 2:
                    s.wait(sync, ("ztfree", i - 2))
                if i == 0:
                    s.wait(sync, ("zw", s.NT // 2 - 1))
                s.inc(s.dZl, sync.dma_start(
                    out=s.zt[:, i % 2, :, :],
                    in_=s.zfm_d.ap()[:, :, t8 * 256:(t8 + 1) * 256]),
                    ("ztload", i), 16)
        s.wait(sync, ("accdone",))
        for t in range(s.NT):
            if t >= 2:
                s.wait(sync, ("xadd", t - 2))
            else:
                s.wait(sync, ("zdone", s.NT - 2 + t))
                s.wait(sync, ("sq", s.NT - 2 + t))
            if t >= 1:
                s.wait(sync, ("ctr", t - 1))
            s.inc(s.dAl, sync.dma_start(
                out=s.accsb[:, 0, :, :],
                in_=s.acc_d.ap()[:, :, t * 128:(t + 1) * 128].transpose([1, 0, 2])),
                None, 16)
            s.inc(s.dAl, sync.dma_start(
                out=s.xt[:, t % 2, :], in_=s.x_d[t * 128:(t + 1) * 128, :]),
                ("cload", t), 16)
            s.wait(sync, ("xadd", t))
            s.inc(s.dXo, sync.dma_start(
                out=s.xo_d[t * 128:(t + 1) * 128, :], in_=s.xt[:, t % 2, :]),
                ("xstore", t), 16)
        if s.debug:
            s.wait(sync, ("selfm", s.NT - 1))
            s.inc(s.dXo, sync.dma_start(out=s.self_d, in_=s.selfm[:]), None, 16)
        if self.emit:
            sync.wait_ge(s.dXo.h, s.dXo.v)

    @engine("gpsimd")
    def b_gpsimd(self, g):
        s = self
        s.inc(s.dC, g.dma_start(out=s.wrsb[:], in_=s.wr_d), ("wrc",), 16)
        for e in range(NE):
            if e >= 1:
                s.wait(g, ("wfree", e - 1))
            for k in range(8):
                s.inc(s.dW, g.dma_start(
                    out=s.w1sb[:, k, :],
                    in_=s.w1_d[e, k * 128:(k + 1) * 128, :]), None, 16)
            for f in range(32):
                key = ("wload", e) if f == 31 else None
                s.inc(s.dW, g.dma_start(
                    out=s.w2sb[:, f, :],
                    in_=s.w2_d[e, f * 128:(f + 1) * 128, :]), key, 16)
            for t8 in range(s.NT8):
                s.wait(g, ("ysb", e, t8))
                op = ALU.bypass if e == 0 else ALU.add
                key = ("accdone",) if (e == NE - 1 and t8 == s.NT8 - 1) else \
                      ("yfree", e, t8)
                s.inc(s.dA, g.dma_start(
                    out=s.acc_d.ap()[:, :, t8 * 256:(t8 + 1) * 256]
                        .transpose([1, 0, 2]),
                    in_=s.ysb[:, t8 % 2, :, :], accum_op=op), key, 16)
                if key == ("accdone",):
                    s.ck[("yfree", e, t8)] = s.ck[key]

    @engine("vector")
    def b_vector(self, v):
        s = self
        s.wait(v, ("consts",))
        C = s.cols
        s.inc(s.sV, v.memset(s.epsc[:], EPS), ("epsc",))
        for t in range(s.NT):
            s.wait(v, ("xload", t))
            s.gsync(v, s.sV, v.tensor_reduce(out=C[:, 0:1], in_=s.xt[:, t % 2, :],
                                             axis=AX.X, op=ALU.add))
            s.wait(v, ("sq", t))
            s.gsync(v, s.sV, v.tensor_scalar(out=C[:, 2:3], in0=C[:, 0:1],
                                             scalar1=1.0 / D, scalar2=0.0,
                                             op0=ALU.mult, op1=ALU.add))
            v.tensor_tensor(out=C[:, 3:4], in0=C[:, 2:3], in1=C[:, 2:3],
                            op=ALU.mult)
            s.gsync(v, s.sV, v.tensor_scalar(out=C[:, 4:5], in0=C[:, 1:2],
                                             scalar1=1.0 / D, scalar2=0.0,
                                             op0=ALU.mult, op1=ALU.add))
            s.inc(s.sV, v.tensor_tensor(out=C[:, 5:6], in0=C[:, 4:5],
                                        in1=C[:, 3:4], op=ALU.subtract),
                  ("var", t))
            s.wait(v, ("rstd", t))
            if t >= 1:
                s.wait(v, ("ztr", t - 1))
            s.inc(s.sV, v.tensor_scalar(out=s.ztile[:], in0=s.xt[:, t % 2, :],
                                        scalar1=C[:, 2:3], scalar2=C[:, 7:8],
                                        op0=ALU.subtract, op1=ALU.mult),
                  ("zdone", t))
            s.wait(v, ("ztr", t))
            if t >= 2:
                s.wait(v, ("zw", (t - 2) // 2))
            s.inc(s.sV, v.tensor_copy(
                out=s.zsb[:, :, (t % 2) * 128:(t % 2) * 128 + 128],
                in_=s.pb0[:].rearrange("p (k q) -> p k q", k=8)), ("zsb", t))
            s.wait(v, ("rmm", t))
            s.gsync(v, s.sV, v.tensor_reduce(out=C[:, 8:9], in_=s.pb1[:, 0:8],
                                             axis=AX.X, op=ALU.max))
            v.tensor_scalar(out=C[:, 9:10], in0=C[:, 8:9], scalar1=-1.0,
                            scalar2=0.0, op0=ALU.mult, op1=ALU.add)
            s.inc(s.sV, v.tensor_single_scalar(out=s.ohsel[:], in_=s.pb1[:, 0:8],
                                               scalar=C[:, 8:9], op=ALU.is_ge),
                  ("nmx", t))
            s.wait(v, ("sume", t))
            s.gsync(v, s.sV, v.reciprocal(out=C[:, 11:12], in_=C[:, 10:11]))
            s.gsync(v, s.sV, v.tensor_scalar(out=C[:, 12:13], in0=C[:, 11:12],
                                             scalar1=0.5, scalar2=0.0,
                                             op0=ALU.mult, op1=ALU.add))
            s.inc(s.sV, v.tensor_scalar(out=s.selw[:], in0=s.ohsel[:],
                                        scalar1=C[:, 12:13], scalar2=0.0,
                                        op0=ALU.mult, op1=ALU.add), ("selw", t))
            s.wait(v, ("seltr", t))
            s.inc(s.sV, v.tensor_copy(out=s.selfm[:, t * 128:(t + 1) * 128],
                                      in_=s.pb2[0:8, 0:128]), ("selfm", t))
        for e in range(NE):
            for t8 in range(s.NT8):
                i = e * s.NT8 + t8
                s.wait(v, ("selbc", e, t8))
                s.inc(s.sV, v.tensor_copy(out=s.selb[:], in_=s.pb2[:, 0:256]),
                      ("selb", e, t8))
                if i >= 2:
                    s.wait(v, ("yfree", (i - 2) // s.NT8, (i - 2) % s.NT8))
                for kd in range(8):
                    s.wait(v, ("ymm", e, t8, kd))
                    psy = s.pb5 if kd % 2 == 0 else s.pb6
                    s.gsync(v, s.sV, v.tensor_scalar(
                        out=s.scr[:, 0:256], in0=psy[:, 0:256],
                        scalar1=s.b2sb[:, e, kd:kd + 1], scalar2=0.0,
                        op0=ALU.add, op1=ALU.add))
                    key = ("ysb", e, t8) if kd == 7 else None
                    s.inc(s.sV, v.tensor_tensor(out=s.ysb[:, t8 % 2, kd, :],
                                                in0=s.scr[:, 0:256], in1=s.selb[:],
                                                op=ALU.mult),
                          ("yev", e, t8, kd))
                    if key:
                        s.ck[key] = (s.sV, s.sV.v)
        for t in range(s.NT):
            s.wait(v, ("cload", t))
            s.wait(v, ("ctr", t))
            s.inc(s.sV, v.tensor_tensor(out=s.xt[:, t % 2, :],
                                        in0=s.xt[:, t % 2, :],
                                        in1=s.pb0[:], op=ALU.add), ("xadd", t))

    @engine("scalar")
    def b_scalar(self, sc):
        s = self
        s.wait(sc, ("consts",))
        s.wait(sc, ("epsc",))
        C = s.cols
        for t in range(s.NT):
            s.wait(sc, ("xload", t))
            s.inc(s.sS, sc.activation(out=s.scr2[:], in_=s.xt[:, t % 2, :],
                                      func=AF.Square, accum_out=C[:, 1:2]),
                  ("sq", t))
            s.wait(sc, ("var", t))
            sc.activation(out=C[:, 6:7], in_=C[:, 5:6], func=AF.Ln, bias=s.epsc[:])
            s.inc(s.sS, sc.activation(out=C[:, 7:8], in_=C[:, 6:7], func=AF.Exp,
                                      scale=-0.5), ("rstd", t))
            s.wait(sc, ("nmx", t))
            s.inc(s.sS, sc.activation(out=s.probs8[:], in_=s.pb1[:, 0:8],
                                      func=AF.Exp, bias=C[:, 9:10],
                                      accum_out=C[:, 10:11]), ("sume", t))
        for e in range(NE):
            for t8 in range(s.NT8):
                for f in range(32):
                    s.wait(sc, ("hmm", e, t8, f))
                    psh = s.pb3 if f % 2 == 0 else s.pb4
                    s.inc(s.sS, sc.activation(out=s.hsb[:, f, :],
                                              in_=psh[:, 0:256], func=AF.Gelu,
                                              bias=s.b1sb[:, e, f:f + 1]),
                          ("hev", e, t8, f))

    @engine("tensor")
    def b_tensor(self, t_):
        s = self
        s.wait(t_, ("consts",))
        for t in range(s.NT):
            s.wait(t_, ("zdone", t))
            if t >= 1:
                s.wait(t_, ("zsb", t - 1))
            for k in range(8):
                ins = t_.transpose(
                    out=s.pb0[:, k * 128:(k + 1) * 128],
                    in_=s.ztile[:, k * 128:(k + 1) * 128], identity=s.identb[:])
                if k == 7:
                    s.inc(s.sT, ins, ("ztr", t))
            s.wait(t_, ("zsb", t))
            s.wait(t_, ("wrc",))
            if t >= 1:
                s.wait(t_, ("nmx", t - 1))
                s.wait(t_, ("sume", t - 1))
            for k in range(8):
                ins = t_.matmul(
                    out=s.pb1[:, 0:8],
                    lhsT=s.zsb[:, k, (t % 2) * 128:(t % 2) * 128 + 128],
                    rhs=s.wrsb[:, k, :], start=(k == 0), stop=(k == 7))
                if k == 7:
                    s.inc(s.sT, ins, ("rmm", t))
            s.wait(t_, ("selw", t))
            if t >= 1:
                s.wait(t_, ("selfm", t - 1))
            s.inc(s.sT, t_.transpose(out=s.pb2[0:8, 0:128], in_=s.selw[:],
                                     identity=s.identf[:]), ("seltr", t))
        last_psh = {0: None, 1: None}
        last_psy = {0: None, 1: None}
        for e in range(NE):
            s.wait(t_, ("wload", e))
            for t8 in range(s.NT8):
                i = e * s.NT8 + t8
                s.wait(t_, ("ztload", i))
                if i >= 1:
                    s.wait(t_, ("selb", *divmod(i - 1, s.NT8)))
                else:
                    s.wait(t_, ("selfm", s.NT - 1))
                s.inc(s.sT, t_.matmul(out=s.pb2[:, 0:256],
                                      lhsT=s.eselsb[:, e * 128:(e + 1) * 128],
                                      rhs=s.selfm[:, t8 * 256:(t8 + 1) * 256],
                                      start=True, stop=True), ("selbc", e, t8))
                for f in range(32):
                    psh = s.pb3 if f % 2 == 0 else s.pb4
                    if last_psh[f % 2] is not None:
                        s.wait(t_, last_psh[f % 2])
                    last_psh[f % 2] = ("hev", e, t8, f)
                    for k in range(8):
                        ins = t_.matmul(out=psh[:, 0:256],
                                        lhsT=s.w1sb[:, k, f * 128:(f + 1) * 128],
                                        rhs=s.zt[:, i % 2, k, :],
                                        start=(k == 0), stop=(k == 7))
                        if k == 7:
                            s.inc(s.sT, ins, ("hmm", e, t8, f))
                s.wait(t_, ("hev", e, t8, 31))
                for kd in range(8):
                    psy = s.pb5 if kd % 2 == 0 else s.pb6
                    if last_psy[kd % 2] is not None:
                        s.wait(t_, last_psy[kd % 2])
                    last_psy[kd % 2] = ("yev", e, t8, kd)
                    for f in range(32):
                        ins = t_.matmul(out=psy[:, 0:256],
                                        lhsT=s.w2sb[:, f, kd * 128:(kd + 1) * 128],
                                        rhs=s.hsb[:, f, :],
                                        start=(f == 0), stop=(f == 31))
                        if f == 31:
                            s.inc(s.sT, ins, ("ymm", e, t8, kd))
                key = ("ztfree", i)
                s.ck[key] = (s.sT, s.sT.v)
            s.ck[("wfree", e)] = (s.sT, s.sT.v)
        for t in range(s.NT):
            s.wait(t_, ("cload", t))
            if t >= 1:
                s.wait(t_, ("xadd", t - 1))
            for k in range(8):
                ins = t_.transpose(out=s.pb0[:, k * 128:(k + 1) * 128],
                                   in_=s.accsb[:, 0, k, :],
                                   identity=s.identb[:])
                if k == 7:
                    s.inc(s.sT, ins, ("ctr", t))


def build_ffn0(debug=False):
    nc = bass.Bass("TRN2", target_bir_lowering=False)
    with ExitStack() as es:
        b = FFN0(nc, es, debug=debug)
        blk = es.enter_context(nc.Block())
        b.run(blk)
    return nc


def host_prep_ffn0(inputs, which="L0"):
    d = {}
    if which == "L0":
        wr, w1, w2, b1, b2 = (inputs["L0_ffn_Wr"], inputs["L0_ffn_W1"],
                              inputs["L0_ffn_b1"], None, None)
    d["w1"] = inputs["L0_ffn_W1"]
    d["w2"] = inputs["L0_ffn_W2"]
    d["wr_l"] = np.ascontiguousarray(
        inputs["L0_ffn_Wr"].reshape(8, 128, 8).transpose(1, 0, 2))
    d["b1_l"] = np.ascontiguousarray(
        inputs["L0_ffn_b1"].reshape(8, 32, 128).transpose(2, 0, 1))
    d["b2_l"] = np.ascontiguousarray(
        inputs["L0_ffn_b2"].reshape(8, 8, 128).transpose(2, 0, 1))
    esel = np.zeros((8, 1024), dtype=np.float32)
    for e in range(8):
        esel[e, e * 128:(e + 1) * 128] = 1.0
    import ml_dtypes as _md
    d["esel"] = esel.astype(_md.bfloat16)
    import ml_dtypes
    d["identb"] = np.eye(128, dtype=np.float32).astype(ml_dtypes.bfloat16)
    d["identf"] = np.eye(128, dtype=np.float32)
    return d




class Mamba(Build):
    NT = T // 128
    NJ = 8
    NCH = 32

    def __init__(self, nc, es, debug=False):
        super().__init__(nc, es)
        self.debug = debug
        E = es.enter_context
        self.x_d = nc.dram_tensor("x", [T, D], F32, kind="ExternalInput").ap()
        self.alog_d = nc.dram_tensor("alog_l", [128, 8, 16], F32, kind="ExternalInput").ap()
        self.dskip_d = nc.dram_tensor("dskip_l", [128, 8], F32, kind="ExternalInput").ap()
        self.dtw_d = nc.dram_tensor("dtw_t", [64, 1024], F32, kind="ExternalInput").ap()
        self.dtb_d = nc.dram_tensor("dtb_l", [128, 8], F32, kind="ExternalInput").ap()
        self.xpw_d = nc.dram_tensor("xpw_l", [128, 8, 96], F32, kind="ExternalInput").ap()
        self.idb_d = nc.dram_tensor("identb_d", [128, 128], BF16, kind="ExternalInput").ap()
        self.idf_d = nc.dram_tensor("identf_d", [128, 128], F32, kind="ExternalInput").ap()
        self.xo_d = nc.dram_tensor("xo", [T, D], F32, kind="ExternalOutput").ap()
        kind = "ExternalOutput" if debug else "Internal"
        self.zfm_d = nc.dram_tensor("zfm", [128, 8, T], BF16, kind=kind)
        if debug:
            self.delta_d = nc.dram_tensor("delta_o", [128, 8, T], F32,
                                          kind="ExternalOutput").ap()
            self.ysl_d = nc.dram_tensor("ysl_o", [128, 8, T], F32,
                                        kind="ExternalOutput").ap()
        self.bfl_d = nc.dram_tensor("bfl", [16, 128], BF16, kind="Internal")
        self.cfl_d = nc.dram_tensor("cfl", [16, 128], BF16, kind="Internal")

        self.xt = E(nc.sbuf_tensor("xt", [128, 2, D], F32))
        self.scr2 = E(nc.sbuf_tensor("scr2", [128, D], BF16))
        self.ztile = E(nc.sbuf_tensor("ztile", [128, D], BF16))
        self.zsb = E(nc.sbuf_tensor("zsb", [128, 8, 256], BF16))
        self.cols = E(nc.sbuf_tensor("cols", [128, 16], F32))
        self.epsc = E(nc.sbuf_tensor("epsc", [128, 1], F32))
        self.identb = E(nc.sbuf_tensor("identb", [128, 128], BF16))
        self.identf = E(nc.sbuf_tensor("identf", [128, 128], F32))
        self.onesb = E(nc.sbuf_tensor("onesb", [1, 128], BF16))
        self.asb = E(nc.sbuf_tensor("asb", [128, 8, 16], F32))
        self.dskip = E(nc.sbuf_tensor("dskip", [128, 8], F32))
        self.dtwb = E(nc.sbuf_tensor("dtwb", [64, 1024], BF16))
        self.dtb = E(nc.sbuf_tensor("dtb", [128, 8], F32))
        self.xpwb = E(nc.sbuf_tensor("xpwb", [128, 8, 96], BF16))
        self.dlow = E(nc.sbuf_tensor("dlow", [64, 2, 512], BF16))
        self.bsb = E(nc.sbuf_tensor("bsb", [16, 2, 512], BF16))
        self.csb = E(nc.sbuf_tensor("csb", [16, 2, 512], BF16))
        self.bflat = E(nc.sbuf_tensor("bflat", [1, 2048], BF16))
        self.cflat = E(nc.sbuf_tensor("cflat", [1, 2048], BF16))
        self.zt = E(nc.sbuf_tensor("zt", [128, 2, 8, 512], BF16))
        self.delta = E(nc.sbuf_tensor("delta", [128, 8, 512], F32))
        self.sp0 = E(nc.sbuf_tensor("sp0", [128, 512], F32))
        self.sp1 = E(nc.sbuf_tensor("sp1", [128, 512], F32))
        self.sp2 = E(nc.sbuf_tensor("sp2", [128, 512], F32))
        self.dAe = E(nc.sbuf_tensor("dAe", [128, 8, 128, 16], F32))
        self.duB = E(nc.sbuf_tensor("duB", [128, 8, 128, 16], BF16))
        self.hcar = E(nc.sbuf_tensor("hcar", [128, 8, 16], F32))
        self.du = E(nc.sbuf_tensor("du", [128, 8, 128], F32))
        self.Bb = E(nc.sbuf_tensor("Bb", [128, 16, 128], BF16))
        self.Cb = E(nc.sbuf_tensor("Cb", [128, 16, 128], BF16))
        self.scr_y = E(nc.sbuf_tensor("scr_y", [128, 8, 32, 16], BF16))
        self.ysl = E(nc.sbuf_tensor("ysl", [128, 8, 128], F32))
        self.ymix = E(nc.sbuf_tensor("ymix", [128, 8, 128], F32))

        self.pb0 = E(nc.psum_tensor("pb0", [128, 1024], BF16))
        self.pb1 = E(nc.psum_tensor("pb1", [128, 512], F32))
        self.pb2 = E(nc.psum_tensor("pb2", [128, 512], F32))
        self.pb3 = E(nc.psum_tensor("pb3", [128, 512], F32))
        self.pb4 = E(nc.psum_tensor("pb4", [128, 512], F32))
        self.pb5 = E(nc.psum_tensor("pb5", [128, 512], F32))
        self.pb6 = E(nc.psum_tensor("pb6", [128, 512], F32))

        self.dC = self.sem("dC")
        self.dX = self.sem("dX")
        self.dZw = self.sem("dZw")
        self.dZl = self.sem("dZl")
        self.dF = self.sem("dF")
        self.dXo = self.sem("dXo")
        self.dDb = self.sem("dDb")
        self.sS = self.sem("sS")
        self.sV = self.sem("sV")
        self.sT = self.sem("sT")

    def bodies(self):
        return [self.b_sync, self.b_gpsimd, self.b_vector, self.b_scalar,
                self.b_tensor]

    @engine("sync")
    def b_sync(self, sync):
        s = self
        s.inc(s.dC, sync.dma_start(out=s.asb[:], in_=s.alog_d), None, 16)
        s.inc(s.dC, sync.dma_start(out=s.dskip[:], in_=s.dskip_d), None, 16)
        s.inc(s.dC, sync.dma_start(out=s.dtb[:], in_=s.dtb_d), None, 16)
        s.inc(s.dC, sync.dma_start(out=s.identb[:], in_=s.idb_d), None, 16)
        s.inc(s.dC, sync.dma_start(out=s.identf[:], in_=s.idf_d), ("consts",), 16)
        for t in range(s.NT):
            if t >= 2:
                s.wait(sync, ("zdone", t - 2))
                s.wait(sync, ("sq", t - 2))
            s.inc(s.dX, sync.dma_start(
                out=s.xt[:, t % 2, :], in_=s.x_d[t * 128:(t + 1) * 128, :]),
                ("xload", t), 16)
            if t % 2 == 1:
                s.wait(sync, ("zsb", t))
                s.inc(s.dZw, sync.dma_start(
                    out=s.zfm_d.ap()[:, :, (t - 1) * 128:(t + 1) * 128],
                    in_=s.zsb[:]), ("zw", t // 2), 16)
        for j in range(s.NJ):
            if j >= 2:
                s.wait(sync, ("ztfree", j - 2))
            if j == 0:
                s.wait(sync, ("zw", s.NT // 2 - 1))
            s.inc(s.dZl, sync.dma_start(
                out=s.zt[:, j % 2, :, :],
                in_=s.zfm_d.ap()[:, :, j * 512:(j + 1) * 512]), ("ztload", j), 16)
            for q in range(4):
                c = j * 4 + q
                if c >= 2:
                    s.wait(sync, ("epifree", c - 2))
                else:
                    s.wait(sync, ("zdone", s.NT - 2 + c))
                    s.wait(sync, ("sq", s.NT - 2 + c))
                s.inc(s.dX, sync.dma_start(
                    out=s.xt[:, c % 2, :],
                    in_=s.x_d[c * 128:(c + 1) * 128, :]), ("xload2", c), 16)
                s.wait(sync, ("xadd", c))
                s.inc(s.dXo, sync.dma_start(
                    out=s.xo_d[c * 128:(c + 1) * 128, :], in_=s.xt[:, c % 2, :]),
                    ("xstore", c), 16)
                if s.debug:
                    s.inc(s.dXo, sync.dma_start(
                        out=s.delta_d[:, :, c * 128:(c + 1) * 128],
                        in_=s.delta[:, :, (c % 4) * 128:(c % 4) * 128 + 128]),
                        None, 16)
                    s.inc(s.dXo, sync.dma_start(
                        out=s.ysl_d[:, :, c * 128:(c + 1) * 128],
                        in_=s.ysl[:]), None, 16)
        if self.emit:
            sync.wait_ge(s.dXo.h, s.dXo.v)

    @engine("gpsimd")
    def b_gpsimd(self, g):
        s = self
        s.inc(s.dC, g.dma_start(out=s.dtwb[:], in_=s.dtw_d), None, 16)
        s.inc(s.dC, g.dma_start(out=s.xpwb[:], in_=s.xpw_d), ("wcast",), 16)
        for j in range(s.NJ):
            s.wait(g, ("bcev", j))
            for q in range(4):
                c = j * 4 + q
                if c >= 1:
                    s.wait(g, ("bcmm", c - 1))
                s.inc(s.dF, g.dma_start(
                    out=s.bfl_d.ap(), in_=s.bsb[:, j % 2, q * 128:(q + 1) * 128]), None, 16)
                s.inc(s.dF, g.dma_start(
                    out=s.cfl_d.ap(), in_=s.csb[:, j % 2, q * 128:(q + 1) * 128]), None, 16)
                if self.emit:
                    g.wait_ge(s.dF.h, s.dF.v)
                s.inc(s.dF, g.dma_start(
                    out=s.bflat[:],
                    in_=s.bfl_d.ap().rearrange("a b -> (a b)")[None, :]), None, 16)
                s.inc(s.dF, g.dma_start(
                    out=s.cflat[:],
                    in_=s.cfl_d.ap().rearrange("a b -> (a b)")[None, :]),
                    ("bbload", c), 16)

    @engine("vector")
    def b_vector(self, v):
        s = self
        s.wait(v, ("consts",))
        C = s.cols
        s.inc(s.sV, v.memset(s.epsc[:], EPS), None)
        s.inc(s.sV, v.memset(s.onesb[:], 1.0), None)
        s.inc(s.sV, v.memset(s.hcar[:], 0.0), None)
        s.wait(v, ("aexp",))
        s.inc(s.sV, v.tensor_scalar(out=s.asb[:], in0=s.asb[:], scalar1=-1.0,
                                    scalar2=0.0, op0=ALU.mult, op1=ALU.add),
              ("aneg",))
        for t in range(s.NT):
            s.wait(v, ("xload", t))
            s.gsync(v, s.sV, v.tensor_reduce(out=C[:, 0:1], in_=s.xt[:, t % 2, :],
                                             axis=AX.X, op=ALU.add))
            s.wait(v, ("sq", t))
            s.gsync(v, s.sV, v.tensor_scalar(out=C[:, 2:3], in0=C[:, 0:1],
                                             scalar1=1.0 / D, scalar2=0.0,
                                             op0=ALU.mult, op1=ALU.add))
            v.tensor_tensor(out=C[:, 3:4], in0=C[:, 2:3], in1=C[:, 2:3],
                            op=ALU.mult)
            s.gsync(v, s.sV, v.tensor_scalar(out=C[:, 4:5], in0=C[:, 1:2],
                                             scalar1=1.0 / D, scalar2=0.0,
                                             op0=ALU.mult, op1=ALU.add))
            s.inc(s.sV, v.tensor_tensor(out=C[:, 5:6], in0=C[:, 4:5],
                                        in1=C[:, 3:4], op=ALU.subtract),
                  ("var", t))
            s.wait(v, ("rstd", t))
            if t >= 1:
                s.wait(v, ("ztr", t - 1))
            s.inc(s.sV, v.tensor_scalar(out=s.ztile[:], in0=s.xt[:, t % 2, :],
                                        scalar1=C[:, 2:3], scalar2=C[:, 7:8],
                                        op0=ALU.subtract, op1=ALU.mult),
                  ("zdone", t))
            s.wait(v, ("ztr", t))
            if t >= 2:
                s.wait(v, ("zw", (t - 2) // 2))
            s.inc(s.sV, v.tensor_copy(
                out=s.zsb[:, :, (t % 2) * 128:(t % 2) * 128 + 128],
                in_=s.pb0[:].rearrange("p (k q) -> p k q", k=8)), ("zsb", t))
        for j in range(s.NJ):
            s.wait(v, ("xpmm", j))
            if j >= 2:
                s.wait(v, ("dmm", j - 2, 7))
                s.wait(v, ("bbload", (j - 2) * 4 + 3))
            v.tensor_copy(out=s.dlow[:, j % 2, :], in_=s.pb1[0:64, :])
            v.tensor_copy(out=s.bsb[:, j % 2, :], in_=s.pb2[0:16, :])
            s.inc(s.sV, v.tensor_copy(out=s.csb[:, j % 2, :],
                                      in_=s.pb3[0:16, :]), ("bcev", j))
            for k in range(8):
                s.wait(v, ("dmm", j, k))
                s.inc(s.sV, v.tensor_scalar(
                    out=s.sp0[:], in0=s.pb4[:], scalar1=s.dtb[:, k:k + 1],
                    scalar2=0.0, op0=ALU.add, op1=ALU.add), ("sp0", j, k))
                if self.emit:
                    v.wait_ge(s.sV.h, s.sV.v)
                s.wait(v, ("spl", j, k))
                s.gsync(v, s.sV, v.tensor_scalar_max(out=s.sp2[:], in0=s.sp0[:],
                                                     scalar1=0.0))
                s.inc(s.sV, v.tensor_tensor(out=s.delta[:, k, :], in0=s.sp2[:],
                                            in1=s.sp1[:], op=ALU.add),
                      ("delta", j, k))
            for q in range(4):
                c = j * 4 + q
                ze = s.zt[:, j % 2, :, q * 128:(q + 1) * 128]
                dsl = s.delta[:, :, q * 128:(q + 1) * 128]
                s.wait(v, ("aneg",))
                if c >= 1:
                    s.wait(v, ("ycmul", c - 1))
                s.inc(s.sV, v.tensor_tensor(
                    out=s.dAe[:],
                    in0=dsl.unsqueeze(3).broadcast_to([128, 8, 128, 16]),
                    in1=s.asb[:].unsqueeze(2).broadcast_to([128, 8, 128, 16]),
                    op=ALU.mult), ("dae", c))
                s.gsync(v, s.sV, v.tensor_tensor(out=s.du[:], in0=dsl, in1=ze,
                                                 op=ALU.mult))
                s.wait(v, ("bbev", c))
                if c >= 1:
                    s.wait(v, ("scan", c - 1))
                s.inc(s.sV, v.tensor_tensor(
                    out=s.duB[:],
                    in0=s.du[:].unsqueeze(3).broadcast_to([128, 8, 128, 16]),
                    in1=s.Bb[:].transpose([0, 2, 1]).unsqueeze(1)
                        .broadcast_to([128, 8, 128, 16]),
                    op=ALU.mult), ("dub", c))
                s.wait(v, ("aexpd", c))
                for t in range(128):
                    hprev = s.hcar[:] if t == 0 else s.dAe[:, :, t - 1, :]
                    s.gsync(v, s.sV, v.tensor_tensor(
                        out=s.dAe[:, :, t, :], in0=s.dAe[:, :, t, :],
                        in1=hprev, op=ALU.mult))
                    s.gsync(v, s.sV, v.tensor_tensor(
                        out=s.dAe[:, :, t, :], in0=s.dAe[:, :, t, :],
                        in1=s.duB[:, :, t, :], op=ALU.add))
                s.inc(s.sV, v.tensor_copy(out=s.hcar[:], in_=s.dAe[:, :, 127, :]),
                      ("scan", c))
                s.wait(v, ("cbev", c))
                for qq in range(4):
                    s.gsync(v, s.sV, v.tensor_tensor(
                        out=s.scr_y[:],
                        in0=s.dAe[:, :, qq * 32:(qq + 1) * 32, :],
                        in1=s.Cb[:].transpose([0, 2, 1])[:, qq * 32:(qq + 1) * 32, :]
                            .unsqueeze(1).broadcast_to([128, 8, 32, 16]),
                        op=ALU.mult))
                    key = ("ycmul", c) if qq == 3 else None
                    s.inc(s.sV, v.tensor_reduce(
                        out=s.ysl[:, :, qq * 32:(qq + 1) * 32],
                        in_=s.scr_y[:], axis=AX.X, op=ALU.add), key)
                    if self.emit:
                        v.wait_ge(s.sV.h, s.sV.v)
                s.gsync(v, s.sV, v.tensor_tensor(
                    out=s.ymix[:], in0=ze,
                    in1=s.dskip[:].unsqueeze(2).broadcast_to([128, 8, 128]),
                    op=ALU.mult))
                s.inc(s.sV, v.tensor_tensor(out=s.ymix[:], in0=s.ymix[:],
                                            in1=s.ysl[:], op=ALU.add),
                      ("ymix", c))
                if q == 3:
                    s.ck[("ztfree", j)] = (s.sV, s.sV.v)
                s.wait(v, ("xload2", c))
                s.wait(v, ("etr", c))
                s.inc(s.sV, v.tensor_tensor(out=s.xt[:, c % 2, 0:512],
                                            in0=s.xt[:, c % 2, 0:512],
                                            in1=s.pb1[:], op=ALU.add))
                s.inc(s.sV, v.tensor_tensor(out=s.xt[:, c % 2, 512:1024],
                                            in0=s.xt[:, c % 2, 512:1024],
                                            in1=s.pb2[:], op=ALU.add),
                      ("xadd", c))
                s.ck[("epifree", c)] = (s.sV, s.sV.v)

    @engine("scalar")
    def b_scalar(self, sc):
        s = self
        s.wait(sc, ("consts",))
        C = s.cols
        s.inc(s.sS, sc.activation(out=s.asb[:], in_=s.asb[:], func=AF.Exp),
              ("aexp",))
        for t in range(s.NT):
            s.wait(sc, ("xload", t))
            s.inc(s.sS, sc.activation(out=s.scr2[:], in_=s.xt[:, t % 2, :],
                                      func=AF.Square, accum_out=C[:, 1:2]),
                  ("sq", t))
            s.wait(sc, ("var", t))
            sc.activation(out=C[:, 6:7], in_=C[:, 5:6], func=AF.Ln, bias=s.epsc[:])
            s.inc(s.sS, sc.activation(out=C[:, 7:8], in_=C[:, 6:7], func=AF.Exp,
                                      scale=-0.5), ("rstd", t))
        for j in range(s.NJ):
            for k in range(8):
                s.wait(sc, ("sp0", j, k))
                if (j, k) != (0, 0):
                    s.wait(sc, ("delta", j, k - 1) if k else ("delta", j - 1, 7))
                sc.activation(out=s.sp1[:], in_=s.sp0[:], func=AF.Abs)
                sc.activation(out=s.sp1[:], in_=s.sp1[:], func=AF.Exp, scale=-1.0)
                s.inc(s.sS, sc.activation(out=s.sp1[:], in_=s.sp1[:], func=AF.Ln,
                                          bias=1.0), ("spl", j, k))
            for q in range(4):
                c = j * 4 + q
                for i4 in range(4):
                    s.wait(sc, ("bbm", c, i4))
                    s.inc(s.sS, sc.copy(
                        out=s.Bb[:, 4 * i4:4 * i4 + 4, :].rearrange("p a b -> p (a b)"),
                        in_=(s.pb5 if i4 % 2 == 0 else s.pb6)[:]), ("bbe", c, i4))
                for i4 in range(4):
                    s.wait(sc, ("bcm", c, i4))
                    s.inc(s.sS, sc.copy(
                        out=s.Cb[:, 4 * i4:4 * i4 + 4, :].rearrange("p a b -> p (a b)"),
                        in_=(s.pb5 if i4 % 2 == 0 else s.pb6)[:]), ("cbe", c, i4))
                s.ck[("bbev", c)] = s.ck[("bbe", c, 3)]
                s.ck[("cbev", c)] = s.ck[("cbe", c, 3)]
                s.wait(sc, ("dae", c))
                s.wait(sc, ("dub", c))
                s.inc(s.sS, sc.activation(out=s.dAe[:], in_=s.dAe[:], func=AF.Exp),
                      ("aexpd", c))

    @engine("tensor")
    def b_tensor(self, t_):
        s = self
        s.wait(t_, ("consts",))
        for t in range(s.NT):
            s.wait(t_, ("zdone", t))
            if t >= 1:
                s.wait(t_, ("zsb", t - 1))
            for k in range(8):
                ins = t_.transpose(out=s.pb0[:, k * 128:(k + 1) * 128],
                                   in_=s.ztile[:, k * 128:(k + 1) * 128],
                                   identity=s.identb[:])
                if k == 7:
                    s.inc(s.sT, ins, ("ztr", t))
        s.wait(t_, ("wcast",))
        for j in range(s.NJ):
            s.wait(t_, ("ztload", j))
            if j >= 1:
                s.wait(t_, ("bcev", j - 1))
                s.wait(t_, ("xadd", j * 4 - 1))
            for k in range(8):
                t_.matmul(out=s.pb1[0:64, :], lhsT=s.xpwb[:, k, 0:64],
                          rhs=s.zt[:, j % 2, k, :], start=(k == 0), stop=(k == 7))
            for k in range(8):
                t_.matmul(out=s.pb2[0:16, :], lhsT=s.xpwb[:, k, 64:80],
                          rhs=s.zt[:, j % 2, k, :], start=(k == 0), stop=(k == 7))
            for k in range(8):
                ins = t_.matmul(out=s.pb3[0:16, :], lhsT=s.xpwb[:, k, 80:96],
                                rhs=s.zt[:, j % 2, k, :], start=(k == 0),
                                stop=(k == 7))
                if k == 7:
                    s.inc(s.sT, ins, ("xpmm", j))
            s.wait(t_, ("bcev", j))
            if j >= 1:
                s.wait(t_, ("dae", j * 4 - 1))
            for k in range(8):
                if (j, k) != (0, 0):
                    s.wait(t_, ("sp0", j, k - 1) if k else ("sp0", j - 1, 7))
                s.inc(s.sT, t_.matmul(out=s.pb4[:],
                                      lhsT=s.dtwb[:, k * 128:(k + 1) * 128],
                                      rhs=s.dlow[:, j % 2, :],
                                      start=True, stop=True), ("dmm", j, k))
            for q in range(4):
                c = j * 4 + q
                s.wait(t_, ("bbload", c))
                for i4 in range(4):
                    pb = s.pb5 if i4 % 2 == 0 else s.pb6
                    if i4 >= 2:
                        s.wait(t_, ("bbe", c, i4 - 2))
                    elif c >= 1:
                        s.wait(t_, ("cbe", c - 1, i4 + 2))
                    s.inc(s.sT, t_.matmul(out=pb[:], lhsT=s.onesb[:],
                                          rhs=s.bflat[:, i4 * 512:(i4 + 1) * 512],
                                          start=True, stop=True), ("bbm", c, i4))
                for i4 in range(4):
                    pb = s.pb5 if i4 % 2 == 0 else s.pb6
                    if i4 < 2:
                        s.wait(t_, ("bbe", c, i4 + 2))
                    else:
                        s.wait(t_, ("cbe", c, i4 - 2))
                    key = ("bcmm", c) if i4 == 3 else ("bcm", c, i4)
                    s.inc(s.sT, t_.matmul(out=pb[:], lhsT=s.onesb[:],
                                          rhs=s.cflat[:, i4 * 512:(i4 + 1) * 512],
                                          start=True, stop=True), key)
                    if i4 == 3:
                        s.ck[("bcm", c, 3)] = s.ck[("bcmm", c)]
                s.wait(t_, ("ymix", c))
                if c >= 1:
                    s.wait(t_, ("xadd", c - 1))
                for k in range(8):
                    pst = s.pb1 if k < 4 else s.pb2
                    ins = t_.transpose(
                        out=pst[:, (k % 4) * 128:(k % 4) * 128 + 128],
                        in_=s.ymix[:, k, :], identity=s.identf[:])
                    if k == 7:
                        s.inc(s.sT, ins, ("etr", c))


def build_mamba(debug=False):
    nc = bass.Bass("TRN2", target_bir_lowering=False)
    with ExitStack() as es:
        b = Mamba(nc, es, debug=debug)
        blk = es.enter_context(nc.Block())
        b.run(blk)
    return nc


def host_prep_mamba(inputs):
    d = {}
    d["alog_l"] = np.ascontiguousarray(
        inputs["L0_attn_A_log"].reshape(8, 128, 16).transpose(1, 0, 2))
    d["dskip_l"] = np.ascontiguousarray(
        inputs["L0_attn_D_skip"].reshape(8, 128).T)
    d["dtw_t"] = np.ascontiguousarray(inputs["L0_attn_dt_proj_w"].T)
    d["dtb_l"] = np.ascontiguousarray(
        inputs["L0_attn_dt_proj_b"].reshape(8, 128).T)
    d["xpw_l"] = np.ascontiguousarray(
        inputs["L0_attn_x_proj_w"].T.reshape(8, 128, 96).transpose(1, 0, 2))
    import ml_dtypes
    d["identb_d"] = np.eye(128, dtype=np.float32).astype(ml_dtypes.bfloat16)
    d["identf_d"] = np.eye(128, dtype=np.float32)
    return d



class Attn(Build):
    NT = T // 128
    NT8 = T // 512
    EPS2 = 1e-6

    def __init__(self, nc, es, debug=False):
        super().__init__(nc, es)
        self.debug = debug
        E = es.enter_context
        dram = lambda n, sh, dt=F32, kind="ExternalInput": nc.dram_tensor(n, sh, dt, kind=kind).ap()
        self.x_d = dram("x", [T, D])
        self.wq_d = dram("wq", [D, D])
        self.wk_d = dram("wk", [D, D])
        self.wv_d = dram("wv", [D, D])
        self.wo_d = dram("wo", [D, D])
        self.bqkv_d = dram("bqkv_l", [128, 3, 8])
        self.bo_d = dram("bo_l", [128, 8])
        self.qkvsb_d = dram("qkvsb", [1, 6, 128])
        self.idb_d = dram("identb_d", [128, 128], BF16)
        self.idf_d = dram("identf_d", [128, 128], F32)
        self.xo_d = dram("xo", [T, D], F32, "ExternalOutput")

        self.xt = E(nc.sbuf_tensor("xt", [128, 2, D], F32))
        self.scr2 = E(nc.sbuf_tensor("scr2", [128, D], BF16))
        self.ztile = E(nc.sbuf_tensor("ztile", [128, D], BF16))
        self.cols = E(nc.sbuf_tensor("cols", [128, 16], F32))
        self.epsc = E(nc.sbuf_tensor("epsc", [128, 2], F32))
        self.identb = E(nc.sbuf_tensor("identb", [128, 128], BF16))
        self.identf = E(nc.sbuf_tensor("identf", [128, 128], F32))
        self.onesr = E(nc.sbuf_tensor("onesr", [1, 128], F32))
        self.zall = E(nc.sbuf_tensor("zall", [128, 8, T], BF16))
        self.Oall = E(nc.sbuf_tensor("Oall", [128, 8, T], BF16))
        self.qkvrow = E(nc.sbuf_tensor("qkvrow", [1, 6, 128], F32))
        self.qkvbt = E(nc.sbuf_tensor("qkvbt", [128, 6, 128], BF16))
        self.bqkv = E(nc.sbuf_tensor("bqkv", [128, 3, 8], F32))
        self.bo = E(nc.sbuf_tensor("bo", [128, 8], F32))
        self.wsb = E(nc.sbuf_tensor("wsb", [128, 3, 8, 128], BF16))
        self.wosb = E(nc.sbuf_tensor("wosb", [128, 2, 8, 128], BF16))
        self.mtm = E(nc.sbuf_tensor("mtm", [128, 2, 128], BF16))
        self.sqt = E(nc.sbuf_tensor("sqt", [128, 128], BF16))
        self.stc = E(nc.sbuf_tensor("stc", [128, 16], F32))
        self.q2 = E(nc.sbuf_tensor("q2", [128, T], BF16))
        self.k2 = E(nc.sbuf_tensor("k2", [128, T], BF16))
        self.vtm = E(nc.sbuf_tensor("vtm", [128, 32, 128], BF16))
        self.Ssb = E(nc.sbuf_tensor("Ssb", [128, 4096], mybir.dt.float16))
        self.Psb = E(nc.sbuf_tensor("Psb", [128, 4096], BF16))
        self.PT = E(nc.sbuf_tensor("PT", [128, 32, 128], BF16))
        self.otm = E(nc.sbuf_tensor("otm", [128, 64], BF16))

        self.yfm = self.PT[:].rearrange("p a b -> p (a b)")[:, 0:2048] \
            .rearrange("p (k t) -> p k t", k=8)
        self.pb0 = E(nc.psum_tensor("pb0", [128, 1024], BF16))
        self.pb7 = E(nc.psum_tensor("pb7", [128, 1024], BF16))
        self.pbS0 = E(nc.psum_tensor("pbS0", [128, 512], F32))
        self.pbS1 = E(nc.psum_tensor("pbS1", [128, 512], F32))
        self.pbP = E(nc.psum_tensor("pbP", [128, 512], F32))
        self.pbO = E(nc.psum_tensor("pbO", [128, 512], F32))
        self.pstr = E(nc.psum_tensor("pstr", [128, 1024], F32))

        self.dC = self.sem("dC")
        self.dX = self.sem("dX")
        self.dW = self.sem("dW")
        self.dXo = self.sem("dXo")
        self.sS = self.sem("sS")
        self.sV = self.sem("sV")
        self.sT = self.sem("sT")

    def bodies(self):
        return [self.b_sync, self.b_gpsimd, self.b_vector, self.b_scalar,
                self.b_tensor]

    def seq_i(self, gp, tq, h):
        return (gp * self.NT + tq) * 2 + h

    def from_seq(self, i):
        gp, r = divmod(i, self.NT * 2)
        tq, h = divmod(r, 2)
        return gp, tq, h

    @engine("sync")
    def b_sync(self, sync):
        s = self
        s.inc(s.dC, sync.dma_start(out=s.bqkv[:], in_=s.bqkv_d), None, 16)
        s.inc(s.dC, sync.dma_start(out=s.bo[:], in_=s.bo_d), None, 16)
        s.inc(s.dC, sync.dma_start(out=s.qkvrow[:], in_=s.qkvsb_d), None, 16)
        s.inc(s.dC, sync.dma_start(out=s.identb[:], in_=s.idb_d), None, 16)
        s.inc(s.dC, sync.dma_start(out=s.identf[:], in_=s.idf_d), ("consts",), 16)
        for t in range(s.NT):
            if t >= 2:
                s.wait(sync, ("zdone", t - 2))
                s.wait(sync, ("sq", t - 2))
            s.inc(s.dX, sync.dma_start(
                out=s.xt[:, t % 2, :], in_=s.x_d[t * 128:(t + 1) * 128, :]),
                ("xload", t), 16)
        for c in range(s.NT):
            if c >= 2:
                s.wait(sync, ("epifree", c - 2))
            else:
                s.wait(sync, ("zdone", s.NT - 2 + c))
                s.wait(sync, ("sq", s.NT - 2 + c))
            s.inc(s.dX, sync.dma_start(
                out=s.xt[:, c % 2, :], in_=s.x_d[c * 128:(c + 1) * 128, :]),
                ("xload2", c), 16)
            s.wait(sync, ("xadd", c))
            s.inc(s.dXo, sync.dma_start(
                out=s.xo_d[c * 128:(c + 1) * 128, :], in_=s.xt[:, c % 2, :]),
                ("xstore", c), 16)
        if self.emit:
            sync.wait_ge(s.dXo.h, s.dXo.v)

    @engine("gpsimd")
    def b_gpsimd(self, g):
        s = self
        for gp in range(8):
            if gp >= 1:
                s.wait(g, ("pairdone", gp - 1))
            for mi, wd in enumerate([s.wq_d, s.wk_d, s.wv_d]):
                for k in range(8):
                    key = ("wp", gp) if (mi == 2 and k == 7) else None
                    s.inc(s.dW, g.dma_start(
                        out=s.wsb[:, mi, k, :],
                        in_=wd[k * 128:(k + 1) * 128, gp * 128:(gp + 1) * 128]),
                        key, 16)
        for t4 in range(16):
            for kd in range(8):
                i = t4 * 8 + kd
                if i >= 2:
                    s.wait(g, ("omm", *divmod(i - 2, 8)))
                for oc in range(8):
                    kk = ("wokd", t4, kd) if oc == 7 else None
                    s.inc(s.dW, g.dma_start(
                        out=s.wosb[:, i % 2, oc, :],
                        in_=s.wo_d[oc * 128:(oc + 1) * 128,
                                   kd * 128:(kd + 1) * 128]), kk, 16)

    @engine("vector")
    def b_vector(self, v):
        s = self
        s.wait(v, ("consts",))
        C = s.cols
        s.inc(s.sV, v.memset(s.epsc[:, 0:1], EPS), None)
        s.inc(s.sV, v.memset(s.epsc[:, 1:2], s.EPS2), None)
        s.inc(s.sV, v.memset(s.onesr[:], 1.0), ("ones",))
        s.wait(v, ("qkvbc",))
        s.inc(s.sV, v.tensor_copy(
            out=s.qkvbt[:, 0:4, :].rearrange("p a b -> p (a b)"),
            in_=s.pstr[:, 0:512]), None)
        s.inc(s.sV, v.tensor_scalar(
            out=s.qkvbt[:, 0:2, :].rearrange("p a b -> p (a b)"),
            in0=s.qkvbt[:, 0:2, :].rearrange("p a b -> p (a b)"),
            scalar1=0.125, scalar2=0.0, op0=ALU.mult, op1=ALU.add), ("qkvbcev1",))
        s.wait(v, ("qkvbc2",))
        s.inc(s.sV, v.tensor_copy(
            out=s.qkvbt[:, 4:6, :].rearrange("p a b -> p (a b)"),
            in_=s.pstr[:, 0:256]), ("btready",))
        for t in range(s.NT):
            s.wait(v, ("xload", t))
            s.gsync(v, s.sV, v.tensor_reduce(out=C[:, 0:1], in_=s.xt[:, t % 2, :],
                                             axis=AX.X, op=ALU.add))
            s.wait(v, ("sq", t))
            s.gsync(v, s.sV, v.tensor_scalar(out=C[:, 2:3], in0=C[:, 0:1],
                                             scalar1=1.0 / D, scalar2=0.0,
                                             op0=ALU.mult, op1=ALU.add))
            v.tensor_tensor(out=C[:, 3:4], in0=C[:, 2:3], in1=C[:, 2:3],
                            op=ALU.mult)
            s.gsync(v, s.sV, v.tensor_scalar(out=C[:, 4:5], in0=C[:, 1:2],
                                             scalar1=1.0 / D, scalar2=0.0,
                                             op0=ALU.mult, op1=ALU.add))
            s.inc(s.sV, v.tensor_tensor(out=C[:, 5:6], in0=C[:, 4:5],
                                        in1=C[:, 3:4], op=ALU.subtract),
                  ("var", t))
            s.wait(v, ("rstd", t))
            if t >= 1:
                s.wait(v, ("ztr", t - 1))
            s.inc(s.sV, v.tensor_scalar(out=s.ztile[:], in0=s.xt[:, t % 2, :],
                                        scalar1=C[:, 2:3], scalar2=C[:, 7:8],
                                        op0=ALU.subtract, op1=ALU.mult),
                  ("zdone", t))
            s.wait(v, ("ztr", t))
            s.inc(s.sV, v.tensor_copy(
                out=s.zall[:, :, t * 128:(t + 1) * 128],
                in_=s.pb0[:].rearrange("p (k q) -> p k q", k=8)), ("zsb", t))
        for gp in range(8):
            for mi in range(3):
                dstall = [s.q2, s.k2, s.Psb][mi]
                for t8 in range(s.NT8):
                    s.wait(v, ("pmm", gp, mi, t8))
                    if gp >= 1 and mi == 0 and t8 == 0:
                        s.wait(v, ("smm", gp - 1, s.NT - 1, 1, 7))
                        s.wait(v, ("pvmm", gp - 1, s.NT - 1, 1))
                    if gp >= 1 and mi == 2 and t8 == 0:
                        s.wait(v, ("ptr", gp - 1, s.NT - 1, 1, 3))
                    s.inc(s.sV, v.tensor_scalar(
                        out=dstall[:, t8 * 512:(t8 + 1) * 512],
                        in0=(s.pbS0 if t8 % 2 == 0 else s.pbS1)[:],
                        scalar1=s.bqkv[:, mi, gp:gp + 1], scalar2=0.0,
                        op0=ALU.add, op1=ALU.add), ("pev", gp, mi, t8))
            for t in range(s.NT):
                for mi in range(3):
                    slot = (t * 3 + mi) % 2
                    s.wait(v, ("lntr", gp, t, mi))
                    s.gsync(v, s.sV, v.tensor_copy(
                        out=s.mtm[:, slot, :],
                        in_=s.pb0[:, 0:128]))
                    mt = s.mtm[:, slot, :]
                    mt3 = mt.rearrange("p (h f) -> p h f", h=2)
                    s.gsync(v, s.sV, v.tensor_reduce(out=s.stc[:, 0:2], in_=mt3,
                                                     axis=AX.X, op=ALU.add))
                    s.inc(s.sV, v.nop(), ("mtm", gp, t, mi))
                    s.wait(v, ("lnsq", gp, t, mi))
                    s.gsync(v, s.sV, v.tensor_reduce(
                        out=s.stc[:, 2:4],
                        in_=s.sqt[:].rearrange("p (h f) -> p h f", h=2),
                        axis=AX.X, op=ALU.add))
                    s.gsync(v, s.sV, v.tensor_scalar(out=s.stc[:, 4:6],
                                                     in0=s.stc[:, 0:2],
                                                     scalar1=1.0 / 64,
                                                     scalar2=0.0,
                                                     op0=ALU.mult, op1=ALU.add))
                    v.tensor_tensor(out=s.stc[:, 6:8], in0=s.stc[:, 4:6],
                                    in1=s.stc[:, 4:6], op=ALU.mult)
                    s.gsync(v, s.sV, v.tensor_scalar(out=s.stc[:, 8:10],
                                                     in0=s.stc[:, 2:4],
                                                     scalar1=1.0 / 64,
                                                     scalar2=0.0,
                                                     op0=ALU.mult, op1=ALU.add))
                    s.inc(s.sV, v.tensor_tensor(out=s.stc[:, 10:12],
                                                in0=s.stc[:, 8:10],
                                                in1=s.stc[:, 6:8],
                                                op=ALU.subtract),
                          ("lnvar", gp, t, mi))
                    s.wait(v, ("lnrstd", gp, t, mi))
                    v.tensor_tensor(out=mt3, in0=mt3,
                                    in1=s.stc[:, 4:6].unsqueeze(2)
                                        .broadcast_to([128, 2, 64]),
                                    op=ALU.subtract)
                    s.gsync(v, s.sV, v.tensor_tensor(
                        out=mt3, in0=mt3,
                        in1=s.stc[:, 12:14].unsqueeze(2).broadcast_to([128, 2, 64]),
                        op=ALU.mult))
                    v.tensor_tensor(out=mt, in0=mt, in1=s.qkvbt[:, 2 * mi, :],
                                    op=ALU.mult)
                    s.inc(s.sV, v.tensor_tensor(out=mt, in0=mt,
                                                in1=s.qkvbt[:, 2 * mi + 1, :],
                                                op=ALU.add), ("lnn", gp, t, mi))
                    if mi == 2:
                        if gp >= 1 and t == 0:
                            s.wait(v, ("pvmm", gp - 1, s.NT - 1, 1))
                        s.inc(s.sV, v.tensor_copy(out=s.vtm[:, t, :], in_=mt),
                              ("vtm", gp, t))
                    else:
                        s.wait(v, ("lnbtr", gp, t, mi))
                        dst = s.q2 if mi == 0 else s.k2
                        s.inc(s.sV, v.tensor_copy(
                            out=dst[:, t * 128:(t + 1) * 128],
                            in_=s.pb7[:, 0:128]),
                            ("qk2", gp, t, mi))
            for tq in range(s.NT):
                for h in range(2):
                    i = s.seq_i(gp, tq, h)
                    s.wait(v, ("sev", gp, tq, h))
                    s.gsync(v, s.sV, v.tensor_reduce(out=C[:, 8:9], in_=s.Ssb[:],
                                                     axis=AX.X, op=ALU.max))
                    s.inc(s.sV, v.tensor_scalar(out=C[:, 9:10], in0=C[:, 8:9],
                                                scalar1=-1.0, scalar2=0.0,
                                                op0=ALU.mult, op1=ALU.add),
                          ("snmx", gp, tq, h))
                    s.wait(v, ("pexp", gp, tq, h))
                    s.gsync(v, s.sV, v.reciprocal(out=C[:, 11:12], in_=C[:, 10:11]))
                    s.inc(s.sV, v.tensor_scalar(out=s.Psb[:], in0=s.Psb[:],
                                                scalar1=C[:, 11:12], scalar2=0.0,
                                                op0=ALU.mult, op1=ALU.add),
                          ("pnorm", gp, tq, h))
                    for pg in range(4):
                        s.wait(v, ("ptr", gp, tq, h, pg))
                        if i > 0 and pg == 0:
                            s.wait(v, ("pvmm", *self.from_seq(i - 1)))
                        s.inc(s.sV, v.tensor_copy(
                            out=s.PT[:, pg * 8:(pg + 1) * 8, :]
                                .rearrange("p a b -> p (a b)"),
                            in_=s.pb0[:]), ("ptev", gp, tq, h, pg))
                    s.wait(v, ("pvmm", gp, tq, h))
                    if h == 0:
                        s.inc(s.sV, v.tensor_copy(
                            out=s.Oall[0:64, gp, tq * 128:(tq + 1) * 128],
                            in_=s.pbO[0:64, 0:128]), ("oev", gp, tq, h))
                    else:
                        s.gsync(v, s.sV, v.tensor_copy(out=s.otm[:],
                                                       in_=s.pbP[:, 0:64]))
                        s.inc(s.sV, v.nop(), ("otmr", gp, tq))
                        s.wait(v, ("otr", gp, tq))
                        s.inc(s.sV, v.tensor_copy(
                            out=s.Oall[64:128, gp, tq * 128:(tq + 1) * 128],
                            in_=s.pb7[64:128, 256:384]), ("oev", gp, tq, h))
            s.inc(s.sV, v.nop(), ("pairdone", gp))
        for t4 in range(16):
            for kd in range(8):
                s.wait(v, ("omm", t4, kd))
                if kd == 0 and t4 >= 1:
                    s.wait(v, ("etr", t4 * 2 - 1))
                s.inc(s.sV, v.tensor_scalar(
                    out=s.yfm[:, kd, :],
                    in0=(s.pbS0 if kd % 2 == 0 else s.pbS1)[:, 0:256],
                    scalar1=s.bo[:, kd:kd + 1], scalar2=0.0,
                    op0=ALU.add, op1=ALU.add), ("yev", t4, kd))
            for tb in range(2):
                c = t4 * 2 + tb
                s.wait(v, ("xload2", c))
                s.wait(v, ("etr", c))
                s.inc(s.sV, v.tensor_tensor(out=s.xt[:, c % 2, :],
                                            in0=s.xt[:, c % 2, :],
                                            in1=s.pb7[:], op=ALU.add),
                      ("xadd", c))
                s.ck[("epifree", c)] = (s.sV, s.sV.v)

    @engine("scalar")
    def b_scalar(self, sc):
        s = self
        s.wait(sc, ("consts",))
        C = s.cols
        for t in range(s.NT):
            s.wait(sc, ("xload", t))
            s.inc(s.sS, sc.activation(out=s.scr2[:], in_=s.xt[:, t % 2, :],
                                      func=AF.Square, accum_out=C[:, 1:2]),
                  ("sq", t))
            s.wait(sc, ("var", t))
            sc.activation(out=C[:, 6:7], in_=C[:, 5:6], func=AF.Ln,
                          bias=s.epsc[:, 0:1])
            s.inc(s.sS, sc.activation(out=C[:, 7:8], in_=C[:, 6:7], func=AF.Exp,
                                      scale=-0.5), ("rstd", t))
        for gp in range(8):
            for t in range(s.NT):
                for mi in range(3):
                    s.wait(sc, ("mtm", gp, t, mi))
                    s.inc(s.sS, sc.activation(out=s.sqt[:],
                                              in_=s.mtm[:, (t * 3 + mi) % 2, :],
                                              func=AF.Square),
                          ("lnsq", gp, t, mi))
                    s.wait(sc, ("lnvar", gp, t, mi))
                    sc.activation(out=s.stc[:, 12:14], in_=s.stc[:, 10:12],
                                  func=AF.Ln, bias=s.epsc[:, 1:2])
                    s.inc(s.sS, sc.activation(out=s.stc[:, 12:14],
                                              in_=s.stc[:, 12:14], func=AF.Exp,
                                              scale=-0.5), ("lnrstd", gp, t, mi))
            for tq in range(s.NT):
                for h in range(2):
                    i = s.seq_i(gp, tq, h)
                    for kc8 in range(8):
                        s.wait(sc, ("smm", gp, tq, h, kc8))
                        key = ("sev", gp, tq, h) if kc8 == 7 else \
                            ("sck", gp, tq, h, kc8)
                        s.inc(s.sS, sc.copy(
                            out=s.Ssb[:, kc8 * 512:(kc8 + 1) * 512],
                            in_=(s.pbS0 if kc8 % 2 == 0 else s.pbS1)[:]), key)
                        if kc8 == 7:
                            s.ck[("sck", gp, tq, h, 7)] = s.ck[key]
                    s.wait(sc, ("snmx", gp, tq, h))
                    if i > 0:
                        s.wait(sc, ("ptr", *self.from_seq(i - 1), 3))
                    s.inc(s.sS, sc.activation(out=s.Psb[:], in_=s.Ssb[:],
                                              func=AF.Exp, bias=C[:, 9:10],
                                              accum_out=C[:, 10:11]),
                          ("pexp", gp, tq, h))

    @engine("tensor")
    def b_tensor(self, t_):
        s = self
        s.wait(t_, ("consts",))
        s.wait(t_, ("ones",))
        ins = None
        for i in range(4):
            ins = t_.matmul(out=s.pstr[:, i * 128:(i + 1) * 128], lhsT=s.onesr[:],
                            rhs=s.qkvrow[:, i, :], start=True, stop=True)
        s.inc(s.sT, ins, ("qkvbc",))
        s.wait(t_, ("qkvbcev1",))
        for i in range(2):
            ins = t_.matmul(out=s.pstr[:, i * 128:(i + 1) * 128], lhsT=s.onesr[:],
                            rhs=s.qkvrow[:, 4 + i, :], start=True, stop=True)
        s.inc(s.sT, ins, ("qkvbc2",))
        for t in range(s.NT):
            s.wait(t_, ("zdone", t))
            if t >= 1:
                s.wait(t_, ("zsb", t - 1))
            for k in range(8):
                ins = t_.transpose(out=s.pb0[:, k * 128:(k + 1) * 128],
                                   in_=s.ztile[:, k * 128:(k + 1) * 128],
                                   identity=s.identb[:])
                if k == 7:
                    s.inc(s.sT, ins, ("ztr", t))
        last_S = {0: None, 1: None}
        for gp in range(8):
            s.wait(t_, ("wp", gp))
            for mi in range(3):
                for t8 in range(s.NT8):
                    par = t8 % 2
                    pb = s.pbS0 if par == 0 else s.pbS1
                    if last_S[par] is not None:
                        s.wait(t_, last_S[par])
                    last_S[par] = ("pev", gp, mi, t8)
                    for k in range(8):
                        ins = t_.matmul(out=pb[:], lhsT=s.wsb[:, mi, k, :],
                                        rhs=s.zall[:, k, t8 * 512:(t8 + 1) * 512],
                                        start=(k == 0), stop=(k == 7))
                        if k == 7:
                            s.inc(s.sT, ins, ("pmm", gp, mi, t8))
            for t in range(s.NT):
                for mi in range(3):
                    slot = (t * 3 + mi) % 2
                    s.wait(t_, ("pev", gp, mi, t // 4))
                    if gp == 0 and t == 0 and mi == 0:
                        s.wait(t_, ("zsb", s.NT - 1))
                    if gp >= 1 and t == 0 and mi == 0:
                        s.wait(t_, ("ptev", gp - 1, s.NT - 1, 1, 3))
                        s.wait(t_, ("otmr", gp - 1, s.NT - 1))
                        s.wait(t_, ("oev", gp - 1, s.NT - 1, 1))
                    if (t * 3 + mi) >= 1:
                        i = t * 3 + mi - 1
                        s.wait(t_, ("mtm", gp, i // 3, i % 3))
                    elif gp >= 1:
                        s.wait(t_, ("mtm", gp - 1, s.NT - 1, 2))
                    src = [s.q2, s.k2, s.Psb][mi]
                    s.inc(s.sT, t_.transpose(
                        out=s.pb0[:, 0:128],
                        in_=src[:, t * 128:(t + 1) * 128],
                        identity=s.identb[:]), ("lntr", gp, t, mi))
                    if mi < 2:
                        s.wait(t_, ("lnn", gp, t, mi))
                        if (t * 2 + mi) >= 1:
                            i = t * 2 + mi - 1
                            s.wait(t_, ("qk2", gp, i // 2, i % 2))
                        elif gp >= 1:
                            s.wait(t_, ("qk2", gp - 1, s.NT - 1, 1))
                        s.inc(s.sT, t_.transpose(
                            out=s.pb7[:, 0:128],
                            in_=s.mtm[:, slot, :],
                            identity=s.identb[:]), ("lnbtr", gp, t, mi))
            s.wait(t_, ("qk2", gp, s.NT - 1, 1))
            s.wait(t_, ("vtm", gp, s.NT - 1))
            for tq in range(s.NT):
                for h in range(2):
                    i = s.seq_i(gp, tq, h)
                    for kc8 in range(8):
                        par = kc8 % 2
                        pb = s.pbS0 if par == 0 else s.pbS1
                        if last_S[par] is not None:
                            s.wait(t_, last_S[par])
                        last_S[par] = ("sck", gp, tq, h, kc8)
                        s.inc(s.sT, t_.matmul(
                            out=pb[:],
                            lhsT=s.q2[64 * h:64 * h + 64, tq * 128:(tq + 1) * 128],
                            rhs=s.k2[64 * h:64 * h + 64, kc8 * 512:(kc8 + 1) * 512],
                            start=True, stop=True), ("smm", gp, tq, h, kc8))
                    s.wait(t_, ("pnorm", gp, tq, h))
                    if tq == 0 and h == 0:
                        s.wait(t_, ("mtm", gp, s.NT - 1, 2))
                    for pg in range(4):
                        if i > 0 or pg > 0:
                            pv = (gp, tq, h, pg - 1) if pg else \
                                (*self.from_seq(i - 1), 3)
                            s.wait(t_, ("ptev", *pv))
                        for k8 in range(8):
                            kc = pg * 8 + k8
                            ins = t_.transpose(
                                out=s.pb0[:, k8 * 128:(k8 + 1) * 128],
                                in_=s.Psb[:, kc * 128:(kc + 1) * 128],
                                identity=s.identb[:])
                        s.inc(s.sT, ins, ("ptr", gp, tq, h, pg))
                    s.wait(t_, ("ptev", gp, tq, h, 3))
                    if h == 0:
                        if i > 0:
                            s.wait(t_, ("oev", *self.from_seq(i - 1)))
                        for kc in range(32):
                            ins = t_.matmul(out=s.pbO[0:64, 0:128],
                                            lhsT=s.vtm[:, kc, 0:64],
                                            rhs=s.PT[:, kc, :],
                                            start=(kc == 0), stop=(kc == 31))
                        s.inc(s.sT, ins, ("pvmm", gp, tq, h))
                    else:
                        if tq > 0 or gp > 0:
                            s.wait(t_, ("otmr", *divmod(i - 2, s.NT * 2)[0:1],
                                        tq - 1 if tq else s.NT - 1)) \
                                if False else None
                        prev = (gp, tq - 1) if tq else (gp - 1, s.NT - 1)
                        if not (gp == 0 and tq == 0):
                            s.wait(t_, ("otmr", *prev))
                        for kc in range(32):
                            ins = t_.matmul(out=s.pbP[:, 0:64],
                                            lhsT=s.PT[:, kc, :],
                                            rhs=s.vtm[:, kc, 64:128],
                                            start=(kc == 0), stop=(kc == 31))
                        s.inc(s.sT, ins, ("pvmm", gp, tq, h))
                        s.wait(t_, ("otmr", gp, tq))
                        s.wait(t_, ("oev", gp, tq, 0))
                        s.inc(s.sT, t_.transpose(out=s.pb7[64:128, 256:384],
                                                 in_=s.otm[:],
                                                 identity=s.identb[:]),
                              ("otr", gp, tq))
        for t4 in range(16):
            for kd in range(8):
                par = kd % 2
                pb = s.pbS0 if par == 0 else s.pbS1
                s.wait(t_, ("wokd", t4, kd))
                if last_S[par] is not None:
                    s.wait(t_, last_S[par])
                last_S[par] = ("yev", t4, kd)
                for oc in range(8):
                    ins = t_.matmul(out=pb[:, 0:256],
                                    lhsT=s.wosb[:, (t4 * 8 + kd) % 2, oc, :],
                                    rhs=s.Oall[:, oc, t4 * 256:(t4 + 1) * 256],
                                    start=(oc == 0), stop=(oc == 7))
                    if oc == 7:
                        s.inc(s.sT, ins, ("omm", t4, kd))
            for tb in range(2):
                c = t4 * 2 + tb
                s.wait(t_, ("yev", t4, 7))
                if c >= 1:
                    s.wait(t_, ("xadd", c - 1))
                for k in range(8):
                    ins = t_.transpose(out=s.pb7[:, ((k % 4) + 4 * (k // 4)) * 128:
                                                ((k % 4) + 4 * (k // 4)) * 128 + 128],
                                       in_=s.yfm[:, k, tb * 128:(tb + 1) * 128],
                                       identity=s.identb[:])
                    if k == 7:
                        s.inc(s.sT, ins, ("etr", c))


def build_attn(debug=False):
    nc = bass.Bass("TRN2", target_bir_lowering=False)
    with ExitStack() as es:
        b = Attn(nc, es, debug=debug)
        blk = es.enter_context(nc.Block())
        b.run(blk)
    return nc


def host_prep_attn(inputs):
    d = {}
    d["wq"] = inputs["L1_attn_Wq"]
    d["wk"] = inputs["L1_attn_Wk"]
    d["wv"] = inputs["L1_attn_Wv"]
    d["wo"] = inputs["L1_attn_Wo"]
    bq = inputs["L1_attn_bq"].reshape(8, 128).T
    bk = inputs["L1_attn_bk"].reshape(8, 128).T
    bv = inputs["L1_attn_bv"].reshape(8, 128).T
    d["bqkv_l"] = np.ascontiguousarray(np.stack([bq, bk, bv], axis=1))
    d["bo_l"] = np.ascontiguousarray(inputs["L1_attn_bo"].reshape(8, 128).T)
    rows = []
    for nm in ["q_scale", "q_bias", "k_scale", "k_bias", "v_scale", "v_bias"]:
        a = inputs["L1_attn_" + nm]
        rows.append(np.concatenate([a, a]))
    d["qkvsb"] = np.stack(rows)[None, :, :].astype(np.float32)
    import ml_dtypes
    d["identb_d"] = np.eye(128, dtype=np.float32).astype(ml_dtypes.bfloat16)
    d["identf_d"] = np.eye(128, dtype=np.float32)
    return d


class L1A(Build):
    NT = T // 128

    def __init__(self, nc, es):
        super().__init__(nc, es)
        E = es.enter_context
        dram = lambda n, sh, dt=F32, kind="ExternalInput": nc.dram_tensor(n, sh, dt, kind=kind).ap()
        self.x_d = dram("x", [T, D])
        self.gw_d = dram("gate_l", [128, 8, 8])
        self.idb_d = dram("identb_d", [128, 128], BF16)
        self.xo_d = dram("xo", [T, D], F32, "ExternalOutput")
        self.ztm_d = dram("ztm", [T, D], BF16, "ExternalOutput")
        self.sc_d = dram("scores_o", [128, 32, 8], F32, "ExternalOutput")

        self.xt = E(nc.sbuf_tensor("xt", [128, 2, D], F32))
        self.scr2 = E(nc.sbuf_tensor("scr2", [128, D], BF16))
        self.ztile = E(nc.sbuf_tensor("ztile", [128, D], BF16))
        self.z05 = E(nc.sbuf_tensor("z05", [128, D], F32))
        self.zsb = E(nc.sbuf_tensor("zsb", [128, 8, 256], BF16))
        self.cols = E(nc.sbuf_tensor("cols", [128, 16], F32))
        self.epsc = E(nc.sbuf_tensor("epsc", [128, 1], F32))
        self.identb = E(nc.sbuf_tensor("identb", [128, 128], BF16))
        self.gwsb = E(nc.sbuf_tensor("gwsb", [128, 8, 8], BF16))
        self.scsb = E(nc.sbuf_tensor("scsb", [128, 32, 8], F32))
        self.probs8 = E(nc.sbuf_tensor("probs8", [128, 8], F32))
        self.pb0 = E(nc.psum_tensor("pb0", [128, 1024], BF16))
        self.pb1 = E(nc.psum_tensor("pb1", [128, 512], F32))

        self.dC = self.sem("dC")
        self.dX = self.sem("dX")
        self.dZ = self.sem("dZ")
        self.dXo = self.sem("dXo")
        self.sS = self.sem("sS")
        self.sV = self.sem("sV")
        self.sT = self.sem("sT")

    def bodies(self):
        return [self.b_sync, self.b_gpsimd, self.b_vector, self.b_scalar,
                self.b_tensor]

    @engine("sync")
    def b_sync(self, sync):
        s = self
        s.inc(s.dC, sync.dma_start(out=s.identb[:], in_=s.idb_d), ("consts",), 16)
        for t in range(s.NT):
            if t >= 2:
                s.wait(sync, ("xdone", t - 2))
                s.wait(sync, ("sq", t - 2))
            s.inc(s.dX, sync.dma_start(
                out=s.xt[:, t % 2, :], in_=s.x_d[t * 128:(t + 1) * 128, :]),
                ("xload", t), 16)
            s.wait(sync, ("zdone", t))
            s.inc(s.dZ, sync.dma_start(
                out=s.ztm_d[t * 128:(t + 1) * 128, :], in_=s.ztile[:]),
                ("zw", t), 16)
            s.wait(sync, ("xdone", t))
            s.inc(s.dXo, sync.dma_start(
                out=s.xo_d[t * 128:(t + 1) * 128, :], in_=s.xt[:, t % 2, :]),
                ("xstore", t), 16)
        s.wait(sync, ("scall",))
        s.inc(s.dXo, sync.dma_start(out=s.sc_d, in_=s.scsb[:]), None, 16)
        if self.emit:
            sync.wait_ge(s.dXo.h, s.dXo.v)

    @engine("gpsimd")
    def b_gpsimd(self, g):
        s = self
        s.inc(s.dC, g.dma_start(out=s.gwsb[:], in_=s.gw_d), ("gw",), 16)

    @engine("vector")
    def b_vector(self, v):
        s = self
        s.wait(v, ("consts",))
        C = s.cols
        s.inc(s.sV, v.memset(s.epsc[:], EPS), None)
        for t in range(s.NT):
            s.wait(v, ("xload", t))
            s.gsync(v, s.sV, v.tensor_reduce(out=C[:, 0:1], in_=s.xt[:, t % 2, :],
                                             axis=AX.X, op=ALU.add))
            s.wait(v, ("sq", t))
            s.gsync(v, s.sV, v.tensor_scalar(out=C[:, 2:3], in0=C[:, 0:1],
                                             scalar1=1.0 / D, scalar2=0.0,
                                             op0=ALU.mult, op1=ALU.add))
            v.tensor_tensor(out=C[:, 3:4], in0=C[:, 2:3], in1=C[:, 2:3],
                            op=ALU.mult)
            s.gsync(v, s.sV, v.tensor_scalar(out=C[:, 4:5], in0=C[:, 1:2],
                                             scalar1=1.0 / D, scalar2=0.0,
                                             op0=ALU.mult, op1=ALU.add))
            s.inc(s.sV, v.tensor_tensor(out=C[:, 5:6], in0=C[:, 4:5],
                                        in1=C[:, 3:4], op=ALU.subtract),
                  ("var", t))
            s.wait(v, ("rstd", t))
            s.gsync(v, s.sV, v.tensor_scalar(out=C[:, 13:14], in0=C[:, 7:8],
                                             scalar1=0.5, scalar2=0.0,
                                             op0=ALU.mult, op1=ALU.add))
            if t >= 1:
                s.wait(v, ("ztr", t - 1))
                s.wait(v, ("zw", t - 1))
            s.inc(s.sV, v.tensor_scalar(out=s.ztile[:], in0=s.xt[:, t % 2, :],
                                        scalar1=C[:, 2:3], scalar2=C[:, 7:8],
                                        op0=ALU.subtract, op1=ALU.mult),
                  ("zdone", t))
            s.gsync(v, s.sV, v.tensor_scalar(out=s.z05[:], in0=s.xt[:, t % 2, :],
                                             scalar1=C[:, 2:3], scalar2=C[:, 13:14],
                                             op0=ALU.subtract, op1=ALU.mult))
            s.inc(s.sV, v.tensor_tensor(out=s.xt[:, t % 2, :],
                                        in0=s.xt[:, t % 2, :], in1=s.z05[:],
                                        op=ALU.add), ("xdone", t))
            s.wait(v, ("ztr", t))
            s.inc(s.sV, v.tensor_copy(
                out=s.zsb[:, :, (t % 2) * 128:(t % 2) * 128 + 128],
                in_=s.pb0[:].rearrange("p (k q) -> p k q", k=8)), ("zsb", t))
            s.wait(v, ("rmm", t))
            s.gsync(v, s.sV, v.tensor_reduce(out=C[:, 8:9], in_=s.pb1[:, 0:8],
                                             axis=AX.X, op=ALU.max))
            s.inc(s.sV, v.tensor_scalar(out=C[:, 9:10], in0=C[:, 8:9],
                                        scalar1=-1.0, scalar2=0.0,
                                        op0=ALU.mult, op1=ALU.add), ("nmx", t))
            s.wait(v, ("sume", t))
            s.gsync(v, s.sV, v.reciprocal(out=C[:, 11:12], in_=C[:, 10:11]))
            key = ("scall",) if t == s.NT - 1 else ("sct", t)
            s.inc(s.sV, v.tensor_scalar(out=s.scsb[:, t, :], in0=s.probs8[:],
                                        scalar1=C[:, 11:12], scalar2=0.0,
                                        op0=ALU.mult, op1=ALU.add), key)

    @engine("scalar")
    def b_scalar(self, sc):
        s = self
        s.wait(sc, ("consts",))
        C = s.cols
        for t in range(s.NT):
            s.wait(sc, ("xload", t))
            s.inc(s.sS, sc.activation(out=s.scr2[:], in_=s.xt[:, t % 2, :],
                                      func=AF.Square, accum_out=C[:, 1:2]),
                  ("sq", t))
            s.wait(sc, ("var", t))
            sc.activation(out=C[:, 6:7], in_=C[:, 5:6], func=AF.Ln, bias=s.epsc[:])
            s.inc(s.sS, sc.activation(out=C[:, 7:8], in_=C[:, 6:7], func=AF.Exp,
                                      scale=-0.5), ("rstd", t))
            s.wait(sc, ("nmx", t))
            s.inc(s.sS, sc.activation(out=s.probs8[:], in_=s.pb1[:, 0:8],
                                      func=AF.Exp, bias=C[:, 9:10],
                                      accum_out=C[:, 10:11]), ("sume", t))

    @engine("tensor")
    def b_tensor(self, t_):
        s = self
        s.wait(t_, ("consts",))
        s.wait(t_, ("gw",))
        for t in range(s.NT):
            s.wait(t_, ("zdone", t))
            if t >= 1:
                s.wait(t_, ("zsb", t - 1))
            for k in range(8):
                ins = t_.transpose(out=s.pb0[:, k * 128:(k + 1) * 128],
                                   in_=s.ztile[:, k * 128:(k + 1) * 128],
                                   identity=s.identb[:])
                if k == 7:
                    s.inc(s.sT, ins, ("ztr", t))
            s.wait(t_, ("zsb", t))
            if t >= 1:
                s.wait(t_, ("nmx", t - 1))
                s.wait(t_, ("sume", t - 1))
            for k in range(8):
                ins = t_.matmul(out=s.pb1[:, 0:8],
                                lhsT=s.zsb[:, k, (t % 2) * 128:(t % 2) * 128 + 128],
                                rhs=s.gwsb[:, k, :],
                                start=(k == 0), stop=(k == 7))
                if k == 7:
                    s.inc(s.sT, ins, ("rmm", t))


def build_l1a():
    nc = bass.Bass("TRN2", target_bir_lowering=False)
    with ExitStack() as es:
        b = L1A(nc, es)
        blk = es.enter_context(nc.Block())
        b.run(blk)
    return nc


def host_prep_l1a(inputs):
    d = {}
    d["gate_l"] = np.ascontiguousarray(
        inputs["L1_ffn_gate_w"].T.reshape(8, 128, 8).transpose(1, 0, 2))
    import ml_dtypes
    d["identb_d"] = np.eye(128, dtype=np.float32).astype(ml_dtypes.bfloat16)
    return d


FP16 = mybir.dt.float16
NITER = 36
KTOP = 4096.0


class L1B(Build):
    NT = T // 128
    NT8 = T // 256
    EPS2 = 1e-5

    def __init__(self, nc, es, debug=False):
        super().__init__(nc, es)
        self.debug = debug
        E = es.enter_context
        dram = lambda n, sh, dt=F32, kind="ExternalInput": nc.dram_tensor(n, sh, dt, kind=kind).ap()
        self.x_d = dram("x", [T, D])
        self.ztm_d = dram("ztm", [T, D], BF16)
        self.scl_d = dram("scores_loc", [128, 32, 8])
        self.scb_d = dram("scores_bis", [128, 2048], FP16)
        self.w1_d = dram("fc1t", [NE, D, DF])
        self.w2_d = dram("fc2t", [NE, DF, D])
        self.b1_d = dram("b1s_l", [128, 8, 32])
        self.b2_d = dram("b2s_l", [128, 8, 8])
        self.lnw_d = dram("lnw_row", [1, D])
        self.lnb_d = dram("lnb_row", [1, D])
        self.e8_d = dram("e8", [8, 128])
        self.e8t_d = dram("e8t", [128, 8])
        self.esel_d = dram("esel", [8, 1024], BF16)
        self.idb_d = dram("identb_d", [128, 128], BF16)
        self.idf_d = dram("identf_d", [128, 128], F32)
        self.xo_d = dram("xo", [T, D], F32, "ExternalOutput")
        kind = "ExternalOutput" if debug else "Internal"
        self.yfm_d = nc.dram_tensor("yfm", [128, 8, T], BF16, kind=kind)
        self.acc_d = nc.dram_tensor("acc", [8, 128, T], BF16, kind=kind)
        if debug:
            self.th_d = nc.dram_tensor("th_o", [128, 8], F32, kind="ExternalOutput").ap()
            self.sf_d = nc.dram_tensor("selfm_o", [8, T], BF16, kind="ExternalOutput").ap()

        self.xt = E(nc.sbuf_tensor("xt", [128, 2, D], F32))
        self.scr = E(nc.sbuf_tensor("scr", [128, 256], F32))
        self.scr2 = E(nc.sbuf_tensor("scr2", [128, D], BF16))
        self.ytile = E(nc.sbuf_tensor("ytile", [128, D], BF16))
        self.zsb = E(nc.sbuf_tensor("zsb", [128, 8, 256], BF16))
        self.cols = E(nc.sbuf_tensor("cols", [128, 16], F32))
        self.epsc = E(nc.sbuf_tensor("epsc", [128, 1], F32))
        self.selw = E(nc.sbuf_tensor("selw", [128, 8], F32))
        self.selfm = E(nc.sbuf_tensor("selfm", [8, T], BF16))
        self.eselsb = E(nc.sbuf_tensor("eselsb", [8, 1024], BF16))
        self.b1sb = E(nc.sbuf_tensor("b1sb", [128, 8, 32], F32))
        self.b2sb = E(nc.sbuf_tensor("b2sb", [128, 8, 8], F32))
        self.identb = E(nc.sbuf_tensor("identb", [128, 128], BF16))
        self.identf = E(nc.sbuf_tensor("identf", [128, 128], F32))
        self.onesr = E(nc.sbuf_tensor("onesr", [1, 128], F32))
        self.scloc = E(nc.sbuf_tensor("scloc", [128, 32, 8], F32))
        self.e8 = E(nc.sbuf_tensor("e8sb", [8, 128], F32))
        self.e8t = E(nc.sbuf_tensor("e8tsb", [128, 8], F32))
        self.bis = E(nc.sbuf_tensor("bis", [8, 8], F32))
        self.midc = E(nc.sbuf_tensor("midc", [128, 2], F32))
        self.thbt = E(nc.sbuf_tensor("thbt", [128, 8], F32))
        self.throw = E(nc.sbuf_tensor("throw", [1, 8], F32))
        self.lnrow = E(nc.sbuf_tensor("lnrow", [1, 2, D], F32))
        self.lnwbt = E(nc.sbuf_tensor("lnwbt", [128, D], BF16))
        self.lnbbt = E(nc.sbuf_tensor("lnbbt", [128, D], BF16))
        self.w1sb = E(nc.sbuf_tensor("w1sb", [128, 8, DF], BF16))
        self.w2sb = E(nc.sbuf_tensor("w2sb", [128, 32, D], BF16))
        self.zt = E(nc.sbuf_tensor("zt", [128, 2, 8, 256], BF16))
        self.hsb = E(nc.sbuf_tensor("hsb", [128, 32, 256], BF16))
        self.scbis = self.hsb[:].rearrange("p a b -> p (a b)")[:, 0:2048].bitcast(FP16)
        self.ysb = E(nc.sbuf_tensor("ysb", [128, 2, 8, 256], BF16))
        self.selb = E(nc.sbuf_tensor("selb", [128, 256], F32))
        self.accsb = E(nc.sbuf_tensor("accsb", [128, 1, 8, 128], BF16))

        self.pb0 = E(nc.psum_tensor("pb0", [128, 1024], BF16))
        self.pb1 = E(nc.psum_tensor("pb1", [128, 512], F32))
        self.pb2 = E(nc.psum_tensor("pb2", [128, 512], F32))
        self.pb3 = E(nc.psum_tensor("pb3", [128, 512], F32))
        self.pb4 = E(nc.psum_tensor("pb4", [128, 512], F32))
        self.pb5 = E(nc.psum_tensor("pb5", [128, 512], F32))
        self.pb6 = E(nc.psum_tensor("pb6", [128, 512], F32))

        self.dC = self.sem("dC")
        self.dX = self.sem("dX")
        self.dZw = self.sem("dZw")
        self.dZl = self.sem("dZl")
        self.dW = self.sem("dW")
        self.dA = self.sem("dA")
        self.dAl = self.sem("dAl")
        self.dXo = self.sem("dXo")
        self.sS = self.sem("sS")
        self.sV = self.sem("sV")
        self.sT = self.sem("sT")

    def bodies(self):
        return [self.b_sync, self.b_gpsimd, self.b_vector, self.b_scalar,
                self.b_tensor]

    @engine("sync")
    def b_sync(self, sync):
        s = self
        s.inc(s.dC, sync.dma_start(out=s.b1sb[:], in_=s.b1_d), None, 16)
        s.inc(s.dC, sync.dma_start(out=s.b2sb[:], in_=s.b2_d), None, 16)
        s.inc(s.dC, sync.dma_start(out=s.eselsb[:], in_=s.esel_d), None, 16)
        s.inc(s.dC, sync.dma_start(out=s.identb[:], in_=s.idb_d), None, 16)
        s.inc(s.dC, sync.dma_start(out=s.identf[:], in_=s.idf_d), None, 16)
        s.inc(s.dC, sync.dma_start(out=s.scbis, in_=s.scb_d), None, 16)
        s.inc(s.dC, sync.dma_start(out=s.scloc[:], in_=s.scl_d), None, 16)
        s.inc(s.dC, sync.dma_start(out=s.e8[:], in_=s.e8_d), None, 16)
        s.inc(s.dC, sync.dma_start(out=s.e8t[:], in_=s.e8t_d), None, 16)
        s.inc(s.dC, sync.dma_start(out=s.lnrow[:, 0, :], in_=s.lnw_d), None, 16)
        s.inc(s.dC, sync.dma_start(out=s.lnrow[:, 1, :], in_=s.lnb_d),
              ("consts",), 16)
        for t in range(s.NT):
            if t >= 2:
                s.wait(sync, ("ydone", t - 2))
                s.wait(sync, ("sq", t - 2))
            s.inc(s.dX, sync.dma_start(
                out=s.xt[:, t % 2, 0:512].bitcast(BF16),
                in_=s.ztm_d[t * 128:(t + 1) * 128, :]), ("zload", t), 16)
            if t % 2 == 1:
                s.wait(sync, ("zsb", t))
                s.inc(s.dZw, sync.dma_start(
                    out=s.yfm_d.ap()[:, :, (t - 1) * 128:(t + 1) * 128],
                    in_=s.zsb[:]), ("zw", t // 2), 16)
        for e in range(NE):
            for t8 in range(s.NT8):
                i = e * s.NT8 + t8
                if i >= 2:
                    s.wait(sync, ("ztfree", i - 2))
                if i == 0:
                    s.wait(sync, ("zw", s.NT // 2 - 1))
                s.inc(s.dZl, sync.dma_start(
                    out=s.zt[:, i % 2, :, :],
                    in_=s.yfm_d.ap()[:, :, t8 * 256:(t8 + 1) * 256]),
                    ("ztload", i), 16)
        s.wait(sync, ("accdone",))
        for t in range(s.NT):
            if t >= 2:
                s.wait(sync, ("xadd", t - 2))
            else:
                s.wait(sync, ("ydone", s.NT - 2 + t))
                s.wait(sync, ("sq", s.NT - 2 + t))
            if t >= 1:
                s.wait(sync, ("ctr", t - 1))
            s.inc(s.dAl, sync.dma_start(
                out=s.accsb[:, 0, :, :],
                in_=s.acc_d.ap()[:, :, t * 128:(t + 1) * 128].transpose([1, 0, 2])),
                None, 16)
            s.inc(s.dAl, sync.dma_start(
                out=s.xt[:, t % 2, :], in_=s.x_d[t * 128:(t + 1) * 128, :]),
                ("cload", t), 16)
            s.wait(sync, ("xadd", t))
            s.inc(s.dXo, sync.dma_start(
                out=s.xo_d[t * 128:(t + 1) * 128, :], in_=s.xt[:, t % 2, :]),
                ("xstore", t), 16)
        if s.debug:
            s.wait(sync, ("thready",))
            s.inc(s.dXo, sync.dma_start(out=s.th_d, in_=s.thbt[:]), None, 16)
            s.wait(sync, ("selfm", s.NT - 1))
            s.inc(s.dXo, sync.dma_start(out=s.sf_d, in_=s.selfm[:]), None, 16)
        if self.emit:
            sync.wait_ge(s.dXo.h, s.dXo.v)

    @engine("gpsimd")
    def b_gpsimd(self, g):
        s = self
        for e in range(NE):
            if e >= 1:
                s.wait(g, ("wfree", e - 1))
            for k in range(8):
                s.inc(s.dW, g.dma_start(
                    out=s.w1sb[:, k, :],
                    in_=s.w1_d[e, k * 128:(k + 1) * 128, :]), None, 16)
            for f in range(32):
                key = ("wload", e) if f == 31 else None
                s.inc(s.dW, g.dma_start(
                    out=s.w2sb[:, f, :],
                    in_=s.w2_d[e, f * 128:(f + 1) * 128, :]), key, 16)
            for t8 in range(s.NT8):
                s.wait(g, ("ysb", e, t8))
                op = ALU.bypass if e == 0 else ALU.add
                key = ("accdone",) if (e == NE - 1 and t8 == s.NT8 - 1) else \
                      ("yfree", e, t8)
                s.inc(s.dA, g.dma_start(
                    out=s.acc_d.ap()[:, :, t8 * 256:(t8 + 1) * 256]
                        .transpose([1, 0, 2]),
                    in_=s.ysb[:, t8 % 2, :, :], accum_op=op), key, 16)
                if key == ("accdone",):
                    s.ck[("yfree", e, t8)] = s.ck[key]

    @engine("vector")
    def b_vector(self, v):
        s = self
        s.wait(v, ("consts",))
        C = s.cols
        s.inc(s.sV, v.memset(s.epsc[:], s.EPS2), None)
        s.inc(s.sV, v.memset(s.onesr[:], 1.0), None)
        s.inc(s.sV, v.memset(s.bis[:, 0:1], 0.0), None)
        s.inc(s.sV, v.memset(s.bis[:, 1:2], 1.0), None)
        s.inc(s.sV, v.memset(s.bis[:, 2:3], 0.5), ("bisinit",))
        s.wait(v, ("lnbc", 0))
        s.inc(s.sV, v.tensor_copy(out=s.lnwbt[:, 0:512], in_=s.pb1[:]), None)
        s.inc(s.sV, v.tensor_copy(out=s.lnwbt[:, 512:1024], in_=s.pb2[:]), ("lnev", 0))
        s.wait(v, ("lnbc", 1))
        s.inc(s.sV, v.tensor_copy(out=s.lnbbt[:, 0:512], in_=s.pb1[:]), None)
        s.inc(s.sV, v.tensor_copy(out=s.lnbbt[:, 512:1024], in_=s.pb2[:]), ("lnev", 1))
        for it in range(NITER):
            s.wait(v, ("midbc", it))
            s.gsync(v, s.sV, v.tensor_copy(out=s.midc[:, 0:1], in_=s.pb3[:, 0:1]))
            s.inc(s.sV, v.nop(), ("midc", it))
            s.gsync(v, s.sV, v.tensor_scalar(
                out=s.scr2[:, 0:1024], in0=s.scbis[:, 0:1024],
                scalar1=s.midc[:, 0:1], scalar2=0.0,
                op0=ALU.is_ge, op1=ALU.add, accum_out=C[:, 13:14]))
            s.gsync(v, s.sV, v.tensor_scalar(
                out=s.scr2[:, 0:1024], in0=s.scbis[:, 1024:2048],
                scalar1=s.midc[:, 0:1], scalar2=0.0,
                op0=ALU.is_ge, op1=ALU.add, accum_out=C[:, 14:15]))
            s.inc(s.sV, v.tensor_tensor(out=C[:, 15:16], in0=C[:, 13:14],
                                        in1=C[:, 14:15], op=ALU.add),
                  ("cnt", it))
            s.wait(v, ("cnt8", it))
            s.gsync(v, s.sV, v.tensor_copy(out=s.bis[:, 3:4], in_=s.pb4[0:8, 0:1]))
            s.gsync(v, s.sV, v.tensor_single_scalar(
                out=s.bis[:, 4:5], in_=s.bis[:, 3:4],
                scalar=KTOP, op=ALU.is_ge))
            s.gsync(v, s.sV, v.tensor_scalar(out=s.bis[:, 5:6], in0=s.bis[:, 4:5],
                                             scalar1=-1.0, scalar2=1.0,
                                             op0=ALU.mult, op1=ALU.add))
            v.tensor_tensor(out=s.bis[:, 6:7], in0=s.bis[:, 2:3],
                            in1=s.bis[:, 0:1], op=ALU.subtract)
            s.gsync(v, s.sV, v.tensor_tensor(out=s.bis[:, 7:8], in0=s.bis[:, 2:3],
                                             in1=s.bis[:, 1:2], op=ALU.subtract))
            s.gsync(v, s.sV, v.tensor_tensor(out=s.bis[:, 6:7], in0=s.bis[:, 6:7],
                                             in1=s.bis[:, 4:5], op=ALU.mult))
            s.gsync(v, s.sV, v.tensor_tensor(out=s.bis[:, 7:8], in0=s.bis[:, 7:8],
                                             in1=s.bis[:, 5:6], op=ALU.mult))
            s.gsync(v, s.sV, v.tensor_tensor(out=s.bis[:, 0:1], in0=s.bis[:, 0:1],
                                             in1=s.bis[:, 6:7], op=ALU.add))
            s.gsync(v, s.sV, v.tensor_tensor(out=s.bis[:, 1:2], in0=s.bis[:, 1:2],
                                             in1=s.bis[:, 7:8], op=ALU.add))
            s.gsync(v, s.sV, v.tensor_tensor(out=s.bis[:, 5:6], in0=s.bis[:, 0:1],
                                             in1=s.bis[:, 1:2], op=ALU.add))
            s.inc(s.sV, v.tensor_scalar(out=s.bis[:, 2:3], in0=s.bis[:, 5:6],
                                        scalar1=0.5, scalar2=0.0,
                                        op0=ALU.mult, op1=ALU.add), ("mid", it))
        s.wait(v, ("thtr",))
        s.inc(s.sV, v.tensor_copy(out=s.throw[:], in_=s.pb6[0:1, 0:8]), ("threv",))
        s.wait(v, ("thbc",))
        s.gsync(v, s.sV, v.tensor_copy(out=s.thbt[:], in_=s.pb5[:, 0:8]))
        s.inc(s.sV, v.nop(), ("thready",))
        for t in range(s.NT):
            zt = s.xt[:, t % 2, 0:512].bitcast(BF16)
            s.wait(v, ("zload", t))
            s.gsync(v, s.sV, v.tensor_reduce(out=C[:, 0:1], in_=zt, axis=AX.X,
                                             op=ALU.add))
            s.wait(v, ("sq", t))
            s.gsync(v, s.sV, v.tensor_scalar(out=C[:, 2:3], in0=C[:, 0:1],
                                             scalar1=1.0 / D, scalar2=0.0,
                                             op0=ALU.mult, op1=ALU.add))
            v.tensor_tensor(out=C[:, 3:4], in0=C[:, 2:3], in1=C[:, 2:3],
                            op=ALU.mult)
            s.gsync(v, s.sV, v.tensor_scalar(out=C[:, 4:5], in0=C[:, 1:2],
                                             scalar1=1.0 / D, scalar2=0.0,
                                             op0=ALU.mult, op1=ALU.add))
            s.inc(s.sV, v.tensor_tensor(out=C[:, 5:6], in0=C[:, 4:5],
                                        in1=C[:, 3:4], op=ALU.subtract),
                  ("var", t))
            s.wait(v, ("rstd", t))
            if t >= 1:
                s.wait(v, ("ztr", t - 1))
            s.gsync(v, s.sV, v.tensor_scalar(out=s.ytile[:], in0=zt,
                                             scalar1=C[:, 2:3], scalar2=C[:, 7:8],
                                             op0=ALU.subtract, op1=ALU.mult))
            s.gsync(v, s.sV, v.tensor_tensor(out=s.ytile[:], in0=s.ytile[:],
                                             in1=s.lnwbt[:], op=ALU.mult))
            s.inc(s.sV, v.tensor_tensor(out=s.ytile[:], in0=s.ytile[:],
                                        in1=s.lnbbt[:], op=ALU.add), ("ydone", t))
            s.wait(v, ("ztr", t))
            if t >= 2:
                s.wait(v, ("zw", (t - 2) // 2))
            s.inc(s.sV, v.tensor_copy(
                out=s.zsb[:, :, (t % 2) * 128:(t % 2) * 128 + 128],
                in_=s.pb0[:].rearrange("p (k q) -> p k q", k=8)), ("zsb", t))
            s.wait(v, ("thready",))
            s.gsync(v, s.sV, v.tensor_tensor(out=s.selw[:], in0=s.scloc[:, t, :],
                                             in1=s.thbt[:], op=ALU.is_ge))
            s.gsync(v, s.sV, v.tensor_tensor(out=s.selw[:], in0=s.selw[:],
                                             in1=s.scloc[:, t, :], op=ALU.mult))
            s.inc(s.sV, v.tensor_scalar(out=s.selw[:], in0=s.selw[:],
                                        scalar1=0.5, scalar2=0.0,
                                        op0=ALU.mult, op1=ALU.add), ("selw", t))
            s.wait(v, ("seltr", t))
            s.inc(s.sV, v.tensor_copy(out=s.selfm[:, t * 128:(t + 1) * 128],
                                      in_=s.pb2[0:8, 0:128]), ("selfm", t))
        for e in range(NE):
            for t8 in range(s.NT8):
                i = e * s.NT8 + t8
                s.wait(v, ("selbc", e, t8))
                s.inc(s.sV, v.tensor_copy(out=s.selb[:], in_=s.pb2[:, 0:256]),
                      ("selb", e, t8))
                if i >= 2:
                    s.wait(v, ("yfree", (i - 2) // s.NT8, (i - 2) % s.NT8))
                for kd in range(8):
                    s.wait(v, ("ymm", e, t8, kd))
                    psy = s.pb5 if kd % 2 == 0 else s.pb6
                    s.gsync(v, s.sV, v.tensor_scalar(
                        out=s.scr[:, 0:256], in0=psy[:, 0:256],
                        scalar1=s.b2sb[:, e, kd:kd + 1], scalar2=0.0,
                        op0=ALU.add, op1=ALU.add))
                    key = ("ysb", e, t8) if kd == 7 else None
                    s.inc(s.sV, v.tensor_tensor(out=s.ysb[:, t8 % 2, kd, :],
                                                in0=s.scr[:, 0:256],
                                                in1=s.selb[:], op=ALU.mult),
                          ("yev", e, t8, kd))
                    if key:
                        s.ck[key] = (s.sV, s.sV.v)
        for t in range(s.NT):
            s.wait(v, ("cload", t))
            s.wait(v, ("ctr", t))
            s.inc(s.sV, v.tensor_tensor(out=s.xt[:, t % 2, :],
                                        in0=s.xt[:, t % 2, :],
                                        in1=s.pb0[:], op=ALU.add), ("xadd", t))

    @engine("scalar")
    def b_scalar(self, sc):
        s = self
        s.wait(sc, ("consts",))
        C = s.cols
        for t in range(s.NT):
            s.wait(sc, ("zload", t))
            if t == 0:
                s.wait(sc, ("mid", NITER - 1))
            s.inc(s.sS, sc.activation(out=s.scr2[:],
                                      in_=s.xt[:, t % 2, 0:512].bitcast(BF16),
                                      func=AF.Square, accum_out=C[:, 1:2]),
                  ("sq", t))
            s.wait(sc, ("var", t))
            sc.activation(out=C[:, 6:7], in_=C[:, 5:6], func=AF.Ln, bias=s.epsc[:])
            s.inc(s.sS, sc.activation(out=C[:, 7:8], in_=C[:, 6:7], func=AF.Exp,
                                      scale=-0.5), ("rstd", t))
        for e in range(NE):
            for t8 in range(s.NT8):
                for f in range(32):
                    s.wait(sc, ("hmm", e, t8, f))
                    psh = s.pb3 if f % 2 == 0 else s.pb4
                    s.inc(s.sS, sc.activation(out=s.hsb[:, f, :],
                                              in_=psh[:, 0:256],
                                              func=AF.Gelu_apprx_tanh,
                                              bias=s.b1sb[:, e, f:f + 1]),
                          ("hev", e, t8, f))

    @engine("tensor")
    def b_tensor(self, t_):
        s = self
        s.wait(t_, ("consts",))
        s.wait(t_, ("bisinit",))
        for i in range(2):
            t_.matmul(out=s.pb1[:], lhsT=s.onesr[:], rhs=s.lnrow[:, i, 0:512],
                      start=True, stop=True)
            ins = t_.matmul(out=s.pb2[:], lhsT=s.onesr[:],
                            rhs=s.lnrow[:, i, 512:1024], start=True, stop=True)
            s.inc(s.sT, ins, ("lnbc", i))
            s.wait(t_, ("lnev", i))
        for it in range(NITER):
            if it > 0:
                s.wait(t_, ("mid", it - 1))
            else:
                s.wait(t_, ("bisinit",))
            if it >= 1:
                s.wait(t_, ("midc", it - 1))
            s.inc(s.sT, t_.matmul(out=s.pb3[:, 0:1], lhsT=s.e8[:],
                                  rhs=s.bis[:, 2:3], start=True, stop=True),
                  ("midbc", it))
            s.wait(t_, ("cnt", it))
            s.inc(s.sT, t_.matmul(out=s.pb4[0:8, 0:1], lhsT=s.e8t[:],
                                  rhs=s.cols[:, 15:16], start=True, stop=True),
                  ("cnt8", it))
        s.wait(t_, ("mid", NITER - 1))
        s.inc(s.sT, t_.transpose(out=s.pb6[0:1, 0:8], in_=s.bis[0:8, 0:1],
                                 identity=s.identf[0:8, 0:8]), ("thtr",))
        s.wait(t_, ("threv",))
        s.inc(s.sT, t_.matmul(out=s.pb5[:, 0:8], lhsT=s.onesr[:],
                              rhs=s.throw[:], start=True, stop=True), ("thbc",))
        for t in range(s.NT):
            s.wait(t_, ("ydone", t))
            if t >= 1:
                s.wait(t_, ("zsb", t - 1))
            for k in range(8):
                ins = t_.transpose(out=s.pb0[:, k * 128:(k + 1) * 128],
                                   in_=s.ytile[:, k * 128:(k + 1) * 128],
                                   identity=s.identb[:])
                if k == 7:
                    s.inc(s.sT, ins, ("ztr", t))
            s.wait(t_, ("selw", t))
            if t >= 1:
                s.wait(t_, ("selfm", t - 1))
            s.inc(s.sT, t_.transpose(out=s.pb2[0:8, 0:128], in_=s.selw[:],
                                     identity=s.identf[:]), ("seltr", t))
        last_psh = {0: None, 1: None}
        last_psy = {0: None, 1: None}
        for e in range(NE):
            s.wait(t_, ("wload", e))
            for t8 in range(s.NT8):
                i = e * s.NT8 + t8
                s.wait(t_, ("ztload", i))
                if i >= 1:
                    s.wait(t_, ("selb", *divmod(i - 1, s.NT8)))
                else:
                    s.wait(t_, ("selfm", s.NT - 1))
                s.inc(s.sT, t_.matmul(out=s.pb2[:, 0:256],
                                      lhsT=s.eselsb[:, e * 128:(e + 1) * 128],
                                      rhs=s.selfm[:, t8 * 256:(t8 + 1) * 256],
                                      start=True, stop=True), ("selbc", e, t8))
                for f in range(32):
                    psh = s.pb3 if f % 2 == 0 else s.pb4
                    if last_psh[f % 2] is not None:
                        s.wait(t_, last_psh[f % 2])
                    last_psh[f % 2] = ("hev", e, t8, f)
                    for k in range(8):
                        ins = t_.matmul(out=psh[:, 0:256],
                                        lhsT=s.w1sb[:, k, f * 128:(f + 1) * 128],
                                        rhs=s.zt[:, i % 2, k, :],
                                        start=(k == 0), stop=(k == 7))
                        if k == 7:
                            s.inc(s.sT, ins, ("hmm", e, t8, f))
                s.wait(t_, ("hev", e, t8, 31))
                for kd in range(8):
                    psy = s.pb5 if kd % 2 == 0 else s.pb6
                    if last_psy[kd % 2] is not None:
                        s.wait(t_, last_psy[kd % 2])
                    last_psy[kd % 2] = ("yev", e, t8, kd)
                    for f in range(32):
                        ins = t_.matmul(out=psy[:, 0:256],
                                        lhsT=s.w2sb[:, f, kd * 128:(kd + 1) * 128],
                                        rhs=s.hsb[:, f, :],
                                        start=(f == 0), stop=(f == 31))
                        if f == 31:
                            s.inc(s.sT, ins, ("ymm", e, t8, kd))
                s.ck[("ztfree", i)] = (s.sT, s.sT.v)
            s.ck[("wfree", e)] = (s.sT, s.sT.v)
        for t in range(s.NT):
            s.wait(t_, ("cload", t))
            if t >= 1:
                s.wait(t_, ("xadd", t - 1))
            for k in range(8):
                ins = t_.transpose(out=s.pb0[:, k * 128:(k + 1) * 128],
                                   in_=s.accsb[:, 0, k, :],
                                   identity=s.identb[:])
                if k == 7:
                    s.inc(s.sT, ins, ("ctr", t))

    def pb6_row(self):
        return self.pb6[0:1, 0:8]


def build_l1b(debug=False):
    nc = bass.Bass("TRN2", target_bir_lowering=False)
    with ExitStack() as es:
        b = L1B(nc, es, debug=debug)
        blk = es.enter_context(nc.Block())
        b.run(blk)
    return nc


def host_prep_l1b(inputs):
    d = {}
    d["fc1t"] = np.ascontiguousarray(inputs["L1_ffn_fc1s"].transpose(0, 2, 1))
    d["fc2t"] = np.ascontiguousarray(inputs["L1_ffn_fc2s"].transpose(0, 2, 1))
    d["b1s_l"] = np.ascontiguousarray(
        inputs["L1_ffn_b1s"].reshape(8, 32, 128).transpose(2, 0, 1))
    d["b2s_l"] = np.ascontiguousarray(
        inputs["L1_ffn_b2s"].reshape(8, 8, 128).transpose(2, 0, 1))
    d["lnw_row"] = inputs["L1_ffn_ln_w"][None, :]
    d["lnb_row"] = inputs["L1_ffn_ln_b"][None, :]
    e8 = np.zeros((8, 128), dtype=np.float32)
    for e in range(8):
        e8[e, e::8] = 1.0
    d["e8"] = e8
    d["e8t"] = np.ascontiguousarray(e8.T)
    esel = np.zeros((8, 1024), dtype=np.float32)
    for e in range(8):
        esel[e, e * 128:(e + 1) * 128] = 1.0
    import ml_dtypes
    d["esel"] = esel.astype(ml_dtypes.bfloat16)
    d["identb_d"] = np.eye(128, dtype=np.float32).astype(ml_dtypes.bfloat16)
    d["identf_d"] = np.eye(128, dtype=np.float32)
    return d


def host_prep_scores(sc_list):
    import ml_dtypes
    full = np.concatenate(
        [sc.transpose(1, 0, 2).reshape(4096, 8) for sc in sc_list], axis=0)
    bis = np.ascontiguousarray(
        full.reshape(16, 2048, 8).transpose(0, 2, 1).reshape(128, 2048))
    return bis.astype(np.float16), full


from concourse.bass_utils import run_bass_kernel_spmd

_CACHE = {}


def _get(name, builder):
    if name not in _CACHE:
        _CACHE[name] = builder()
    return _CACHE[name]


def kernel(**inputs):
    xs = [np.ascontiguousarray(inputs["L0_attn_u"][c]) for c in range(8)]

    def run(nc, maps):
        return run_bass_kernel_spmd(nc, maps, core_ids=list(range(8))).results

    def l0ffn(xs):
        nc = _get("ffn0", build_ffn0)
        prep = host_prep_ffn0(inputs)
        prep = {("identb_d" if k == "identb" else "identf_d" if k == "identf"
                 else k): v for k, v in prep.items()}
        res = run(nc, [dict(prep, x=xs[c]) for c in range(8)])
        return [np.asarray(res[c]["xo"]) for c in range(8)]

    def mamba(xs):
        nc = _get("mamba", build_mamba)
        prep = host_prep_mamba(inputs)
        res = run(nc, [dict(prep, x=xs[c]) for c in range(8)])
        return [np.asarray(res[c]["xo"]) for c in range(8)]

    def l1ffn(xs):
        nca = _get("l1a", build_l1a)
        prep_a = host_prep_l1a(inputs)
        res_a = run(nca, [dict(prep_a, x=xs[c]) for c in range(8)])
        sc_list = [np.asarray(res_a[c]["scores_o"]) for c in range(8)]
        bis, full = host_prep_scores(sc_list)
        ncb = _get("l1b", build_l1b)
        prep_b = host_prep_l1b(inputs)
        maps = []
        for c in range(8):
            m = dict(prep_b)
            m["x"] = np.ascontiguousarray(res_a[c]["xo"])
            m["ztm"] = np.ascontiguousarray(res_a[c]["ztm"])
            m["scores_bis"] = bis
            m["scores_loc"] = np.ascontiguousarray(
                full[c * 4096:(c + 1) * 4096].reshape(32, 128, 8)
                .transpose(1, 0, 2))
            maps.append(m)
        res_b = run(ncb, maps)
        return [np.asarray(res_b[c]["xo"]) for c in range(8)]

    def attn(xs):
        nc = _get("attn", build_attn)
        prep = host_prep_attn(inputs)
        res = run(nc, [dict(prep, x=xs[c]) for c in range(8)])
        return [np.asarray(res[c]["xo"]) for c in range(8)]

    xs = l0ffn(xs)
    xs = mamba(xs)
    xs = l0ffn(xs)
    xs = l1ffn(xs)
    xs = attn(xs)
    xs = l1ffn(xs)
    return np.stack(xs).astype(np.float32)
